# version 01
# baseline (speedup 1.0000x reference)
import numpy as np
import ml_dtypes
import concourse.bass as bass
import concourse.mybir as mybir
from concourse.bass_utils import run_bass_kernel_spmd
from contextlib import ExitStack

F32 = mybir.dt.float32
BF16 = mybir.dt.bfloat16
ALU = mybir.AluOpType
ACTF = mybir.ActivationFunctionType
AX = mybir.AxisListType
BF = ml_dtypes.bfloat16

S, D, H = 2048, 1024, 16
TC, TT, DC, FC = 16, 4, 8, 32
F = 4096
NCORE = 8
DEBUG = False
PH = 4
P2SUB = 'full'
NHEAD = H


def _host_consts():
    j = np.arange(32)
    inv = 10000.0 ** (-(j.astype(np.float64)) / 32.0)
    t = np.arange(S, dtype=np.float64)
    ang = t[None, :] * inv[:, None]
    p = np.arange(128)
    coss = np.cos(ang)[p % 32, :]
    sins = np.sin(ang)[p % 32, :] * np.where((p % 64) < 32, -1.0, 1.0)[:, None]
    psig = np.zeros((128, 128))
    psig[p, p ^ 32] = 1.0
    trim = (np.arange(128)[None, :] >= np.arange(128)[:, None]).astype(np.float64)
    return {
        "coss": coss.astype(np.float32).astype(BF),
        "sins": sins.astype(np.float32).astype(BF),
        "psig": psig.astype(BF),
        "trim": trim.astype(BF),
        "onesb": np.ones((128, 128), BF),
        "ones32": np.ones((1, 128), np.float32),
        "idf32": np.eye(128, dtype=np.float32),
        "idbf": np.eye(128, dtype=np.float32).astype(BF),
        "eps5": np.full((128, 1), 1e-5, np.float32),
        "zerob": np.zeros((128, 384), BF),
    }


def _prep_weights(I):
    g = lambda k: np.asarray(I[k], np.float32)
    w = {}
    w["wqT"] = g("L0_attn_Wq").T
    w["wkT"] = g("L0_attn_Wk").T
    w["wvT"] = g("L0_attn_Wv").T
    w["woT"] = g("L0_attn_Wo").T
    w["wiT"] = g("L0_ffn_wi_W").T
    w["fwoT"] = g("L0_ffn_wo_W").T
    w["gwT"] = g("L0_ffn_gate_W").T
    w["acatT"] = g("L0_ffn_lora_A").reshape(32, 1024).T
    w["bcatT"] = g("L0_ffn_lora_B").transpose(1, 0, 2).reshape(4096, 32).T
    w["bq_c"] = g("L0_attn_bq").reshape(16, 128).T
    w["bk_c"] = g("L0_attn_bk").reshape(16, 128).T
    w["bv_r"] = g("L0_attn_bv").reshape(1, 1024)
    w["bo_c"] = g("L0_attn_bo").reshape(8, 128).T
    w["wib_c"] = g("L0_ffn_wi_b").reshape(32, 128).T
    w["wob_c"] = g("L0_ffn_wo_b").reshape(8, 128).T
    w["gb_r"] = g("L0_ffn_gate_b").reshape(1, 8)
    w["hnw_c"] = g("L0_attn_head_norm_w").reshape(1024).reshape(8, 128).T
    w["lamA"] = np.concatenate([g("L0_attn_lq1"), g("L0_attn_lq2")], axis=1)
    w["lamB"] = np.concatenate([g("L0_attn_lk1"), g("L0_attn_lk2")], axis=1)
    return {k: np.ascontiguousarray(v) for k, v in w.items()}


NDMASEM = 8


class Em:
    CSEM = {"T": "tT", "V": "tV", "A": "tA"}

    def __init__(self, mode, sems, events, eng=None):
        self.mode = mode
        self.sems = sems
        self.events = events
        self.eng = eng
        self.counts = {}
        self.qidx = {"S": 0, "G": 0}
        self.waited = {}
        self.own_sems = set()
        if mode in self.CSEM:
            self.own_sems.add(self.CSEM[mode])
        elif mode in ("S", "G"):
            self.own_sems.update(f"t{mode}{i}" for i in range(NDMASEM))

    def op(self, engine, fn):
        if self.mode == engine:
            fn(self.eng)

    def sig(self, engine, fn, evt=None):
        if engine in ("S", "G"):
            slot = self.qidx[engine] % NDMASEM
            self.qidx[engine] += 1
            sem = f"t{engine}{slot}"
            prev = self.counts.get(sem, 0)
            if self.mode == "count":
                self.counts[sem] = prev + 16
                if evt is not None:
                    assert evt not in self.events, f"dup evt {evt}"
                    self.events[evt] = (sem, prev + 16)
            elif self.mode == engine:
                if prev > 0:
                    self.eng.wait_ge(self.sems[sem], prev)
                    self.waited[sem] = max(self.waited.get(sem, -1), prev)
                self.counts[sem] = prev + 16
                fn(self.eng).then_inc(self.sems[sem], 16)
            else:
                self.counts[sem] = prev + 16
            return
        sem = self.CSEM[engine]
        if self.mode == "count":
            self.counts[sem] = self.counts.get(sem, 0) + 1
            if evt is not None:
                assert evt not in self.events, f"dup evt {evt}"
                self.events[evt] = (sem, self.counts[sem])
        elif self.mode == engine:
            ins = fn(self.eng)
            assert ins is not None, "sig() requires a real instruction (nop+inc faults HW)"
            ins.then_inc(self.sems[sem], 1)

    def alias(self, evt_new, evt_old):
        if self.mode == "count":
            assert evt_new not in self.events and evt_old in self.events
            self.events[evt_new] = self.events[evt_old]

    def wait(self, engine, evt, force=False):
        if self.mode == "count":
            assert evt in self.events, f"wait on future evt {evt}"
            return
        if self.mode == engine:
            sem, val = self.events[evt]
            if (not force) and sem in self.own_sems and sem in self.CSEM.values():
                return
            if self.waited.get(sem, -1) >= val:
                return
            self.waited[sem] = val
            self.eng.wait_ge(self.sems[sem], val)


def build_nc():
    nc = bass.Bass("TRN2", target_bir_lowering=False, detect_race_conditions=False)
    dr = {}

    def din(name, shape, dt=F32):
        dr[name] = nc.dram_tensor(name, list(shape), dt, kind="ExternalInput")

    din("x", (S, D)); din("xT", (D, S))
    din("wqT", (D, 2 * D)); din("wkT", (D, 2 * D)); din("wvT", (D, D)); din("woT", (D, D))
    din("wiT", (D, F)); din("fwoT", (F, D)); din("gwT", (D, 8))
    din("acatT", (D, 32)); din("bcatT", (32, F))
    din("bq_c", (128, 16)); din("bk_c", (128, 16)); din("bv_r", (1, D))
    din("bo_c", (128, 8)); din("wib_c", (128, 32)); din("wob_c", (128, 8))
    din("gb_r", (1, 8)); din("hnw_c", (128, 8))
    din("lamA", (16, 128)); din("lamB", (16, 128)); din("eps5", (128, 1))
    din("coss", (128, S), BF16); din("sins", (128, S), BF16)
    din("psig", (128, 128), BF16); din("trim", (128, 128), BF16)
    din("onesb", (128, 128), BF16); din("zerob", (128, 384), BF16); din("ones32", (1, 128)); din("idf32", (128, 128))
    din("idbf", (128, 128), BF16)
    out_h = nc.dram_tensor("outT", [D, S], F32, kind="ExternalOutput")
    dbg = {}
    if DEBUG:
        for nm, shp in [("xnTb", (128, 16384)), ("v", (128, 16640)),
                        ("uT", (32, 2048)), ("qrot", (128, 2048)),
                        ("krot", (128, 2048)), ("att", (128, 16384)),
                        ("sT", (128, 16384)), ("inter", (128, 16384)),
                        ("gmask", (128, 8)), ("xnTb2", (128, 16384)),
                        ("gmask2", (128, 8)), ("uT2", (32, 2048)),
                        ("glog", (128, 8)), ("stat", (128, 8)),
                        ("xnT32", (128, 1024)), ("xn0", (128, 1024)),
                        ("xs0", (128, 1024)), ("st0", (128, 8))]:
            dbg[nm] = nc.dram_tensor("d_" + nm, list(shp), F32, kind="ExternalOutput")

    es = ExitStack()
    sb = {}

    def tile(name, shape, dt):
        h = es.enter_context(nc.sbuf_tensor("t_" + name, list(shape), dt))
        sb[name] = h.ap()
        return h

    tile("xnTb", (128, 16384), BF16)
    tile("attoT", (128, 16384), BF16)
    tile("sTb", (128, 16384), BF16)
    tile("uTb", (32, 2048), BF16)
    vtile_h = tile("vtile", (128, 16640), BF16)
    tile("coss", (128, S), BF16); tile("sins", (128, S), BF16)
    tile("psig", (128, 128), BF16); tile("trim", (128, 128), BF16)
    tile("onesb", (128, 128), BF16); tile("zerob", (128, 384), BF16); tile("ones32", (1, 128), F32)
    tile("idf32", (128, 128), F32); tile("idbf", (128, 128), BF16)
    tile("bqc", (128, 16), F32); tile("bkc", (128, 16), F32)
    tile("bvr", (1, D), BF16); tile("boc", (128, 8), F32)
    tile("wibc", (128, 32), F32); tile("wobc", (128, 8), F32)
    tile("gbr", (1, 8), F32); tile("hnwc", (128, 8), F32)
    tile("lamA", (16, 128), F32); tile("lamB", (16, 128), F32)
    tile("lprod", (16, 128), F32); tile("lsc", (16, 8), F32)
    tile("lbc", (128, 16), F32)
    tile("gwT32", (128, 64), F32)
    tile("acatTb", (128, 256), BF16)
    tile("bctb", (32, F), BF16)
    tile("stat", (128, 8), F32)
    tile("eps5", (128, 1), F32)
    tile("gmask", (128, 8), F32)
    tile("dbgl", (128, 8), F32)
    umb_h = tile("umb", (128, 32), BF16)

    ARN = es.enter_context(nc.sbuf_tensor("ARN", [128, 12800], F32))
    ARNf = ARN.ap()
    ARNb = ARN.ap().bitcast(BF16)

    def af(byte_off, cols):
        assert byte_off % 4 == 0
        return ARNf[:, byte_off // 4: byte_off // 4 + cols]

    def ab(byte_off, cols):
        assert byte_off % 2 == 0
        return ARNb[:, byte_off // 2: byte_off // 2 + cols]

    sb["xst"] = af(0, 2048)
    sb["xsq"] = af(8192, 1024)
    sb["xn"] = af(12288, 1024)
    sb["xnT32"] = af(16384, 1024)
    sb["wvTb"] = ab(20480, 8192)
    sb["wqb"] = ab(0, 2048)
    sb["wkb"] = ab(4096, 2048)
    sb["qsb"] = ab(8192, 512)
    sb["t1b"] = ab(9216, 512)
    sb["t2b"] = ab(10240, 512)
    sb["qrotb"] = ab(11264, 2048)
    sb["krotb"] = ab(15360, 2048)
    sb["e1b"] = ab(19456, 2048)
    sb["e2b"] = ab(23552, 2048)
    sb["rtmp"] = af(27648, 1024)
    sb["rbf"] = ab(31744, 1024)
    sb["ct"] = af(33792, 512)
    sb["cv"] = af(35840, 512)
    sb["v2b"] = ab(37888, 512)
    sb["zt"] = af(38912, 1024)
    sb["m2s"] = af(43008, 512)
    sb["rsb"] = ab(45056, 512)
    sb["atmp"] = ab(46080, 512)
    sb["wost"] = ab(47104, 2048)
    sb["interTb"] = vtile_h.ap()[:, 0:16384]
    sb["wib"] = ab(0, 2048)
    sb["fwoTb"] = ab(4096, 8192)
    sb["gbuf"] = af(20480, 1024)
    sb["tbuf"] = af(24576, 512)
    sb["xTs"] = af(26624, 1024)
    sb["outb"] = af(30720, 1024)

    PB = es.enter_context(nc.psum_tensor("PB", [128, 4096], F32))

    def pbf(c0, c1, p0=0, p1=128):
        return PB[p0:p1, c0:c1]

    def pbb(c0b, c1b, p0=0, p1=128):
        return PB.ap().bitcast(BF16)[p0:p1, c0b:c1b]

    sems = {}
    semnames = ["tT", "tV", "tA"] + [f"tS{i}" for i in range(NDMASEM)] \
        + [f"tG{i}" for i in range(NDMASEM)]
    for sname in semnames:
        sems[sname] = es.enter_context(nc.semaphore(sname))
    events = {}

    def prog(em: Em):
        V, T, A, G, Sq = "V", "T", "A", "G", "S"

        small = [("coss", "coss"), ("sins", "sins"), ("psig", "psig"),
                 ("trim", "trim"), ("onesb", "onesb"), ("ones32", "ones32"),
                 ("idf32", "idf32"), ("idbf", "idbf"),
                 ("bq_c", "bqc"), ("bk_c", "bkc"), ("bo_c", "boc"),
                 ("wib_c", "wibc"), ("wob_c", "wobc"), ("gb_r", "gbr"),
                 ("hnw_c", "hnwc"), ("lamA", "lamA"), ("lamB", "lamB"),
                 ("eps5", "eps5"), ("zerob", "zerob")]
        for i, (nm, dst) in enumerate(small):
            em.sig(Sq, lambda e, nm=nm, dst=dst: e.dma_start(sb[dst][:], dr[nm].ap()),
                   evt=f"lds_{i}")
        em.sig(Sq, lambda e: e.dma_start(
            sb["gwT32"].rearrange("p (a b) -> p a b", a=DC),
            bass.AP(dr["gwT"], 0, [[8, 128], [128 * 8, DC], [1, 8]])), evt="ld_gw")

        def ld_barrier(engine):
            for i in range(len(small) + 1 - NDMASEM, len(small)):
                em.wait(engine, f"lds_{i}")
            em.wait(engine, "ld_gw")
        em.sig(G, lambda e: e.dma_start(
            sb["wvTb"].rearrange("p (a b) -> p a b", a=DC),
            bass.AP(dr["wvT"], 0, [[D, 128], [128 * D, DC], [1, D]])), evt="ld_wv")
        em.sig(G, lambda e: e.dma_start(
            sb["acatTb"].rearrange("p (a b) -> p a b", a=DC),
            bass.AP(dr["acatT"], 0, [[32, 128], [128 * 32, DC], [1, 32]])), evt="ld_acat")
        em.sig(G, lambda e: e.dma_start(sb["bvr"][:], dr["bv_r"].ap()), evt="ld_bv")
        em.sig(G, lambda e: e.dma_start(sb["bctb"][:], dr["bcatT"].ap()), evt="ld_bct")

        ld_barrier("A")

        ld_barrier("V")
        em.op(V, lambda e: e.tensor_scalar_mul(sb["hnwc"][:], sb["hnwc"][:], 0.8))
        em.sig(V, lambda e: e.memset(vtile_h[:], 1.0), evt="v_ones")
        em.op(V, lambda e: e.tensor_mul(sb["lprod"][:], sb["lamA"][:], sb["lamB"][:]))
        em.op(V, lambda e: e.reduce_sum(sb["lsc"][:, 0:1], sb["lprod"][:, 0:64], axis=AX.X))
        em.sig(V, lambda e: e.reduce_sum(sb["lsc"][:, 1:2], sb["lprod"][:, 64:128], axis=AX.X),
               evt="lam_s")
        em.wait(A, "lam_s")
        em.sig(A, lambda e: e.activation(sb["lsc"][:, 2:4], sb["lsc"][:, 0:2], ACTF.Exp),
               evt="lam_e")
        em.wait(V, "lam_e")
        em.sig(V, lambda e: e.scalar_tensor_tensor(
            sb["lsc"][:, 5:6], sb["lsc"][:, 2:3], 0.2, sb["lsc"][:, 3:4],
            ALU.add, ALU.subtract), evt="lam_c")
        ld_barrier("T")
        em.wait(Sq, "lam_c")
        em.sig(Sq, lambda e: e.dma_start(sb["lbc"][64:65, 0:16], sb["lsc"][:, 5:6]),
               evt="lam_bc")

        if PH <= 0:
            em.sig(Sq, lambda e: e.dma_start(
                out_h.ap()[0:128, :],
                sb["xnTb"].bitcast(F32)[:, 0:2048]), evt="dummy_sto")
            em.wait(Sq, "dummy_sto", force=True)
            return
        for tc in range(TC):
            par = tc % 2
            xs = sb["xst"][:, par * D:(par + 1) * D]
            if tc >= 2:
                em.wait(Sq, f"xn_{tc-2}")
            em.sig(Sq, lambda e, tc=tc, xs=xs: e.dma_start(
                xs, dr["x"].ap()[tc * 128:(tc + 1) * 128, :]), evt=f"ldx_{tc}")
            em.wait(V, f"ldx_{tc}")
            st = sb["stat"]
            if tc > 0:
                em.wait(V, f"rstd_{tc-1}")
            em.op(V, lambda e, xs=xs: e.reduce_sum(st[:, 0:1], xs, axis=AX.X))
            em.op(V, lambda e, xs=xs: e.tensor_mul(sb["xsq"][:], xs, xs))
            em.op(V, lambda e: e.tensor_mul(st[:, 2:3], st[:, 0:1], st[:, 0:1]))
            em.op(V, lambda e: e.reduce_sum(st[:, 1:2], sb["xsq"][:], axis=AX.X))
            em.op(V, lambda e: e.tensor_scalar_mul(st[:, 3:4], st[:, 0:1], 1.0 / D))
            em.sig(V, lambda e: e.tensor_scalar(
                st[:, 5:6], st[:, 2:3], -1.0 / (D * D), sb["eps5"][:, 0:1],
                ALU.mult, ALU.add), evt=f"var_{tc}")
            em.wait(A, f"var_{tc}")
            em.op(A, lambda e: e.activation(st[:, 6:7], st[:, 1:2], ACTF.Ln,
                                            scale=1.0 / D, bias=st[:, 5:6]))
            em.sig(A, lambda e: e.activation(st[:, 7:8], st[:, 6:7], ACTF.Exp, scale=-0.5),
                   evt=f"rstd_{tc}")
            em.wait(V, f"rstd_{tc}")
            if tc > 0:
                em.wait(V, f"tr_{tc-1}_7")
            em.sig(V, lambda e, xs=xs: e.tensor_scalar(
                sb["xn"][:], xs, st[:, 3:4], st[:, 7:8], ALU.subtract, ALU.mult),
                evt=f"xn_{tc}")
            if DEBUG and tc == 0:
                em.wait(G, "xn_0")
                em.sig(G, lambda e: e.dma_start(dbg["xn0"].ap(), sb["xn"][:]),
                       evt="dbg_xn0")
                em.sig(G, lambda e: e.dma_start(dbg["xs0"].ap(), sb["xst"][:, 0:1024]),
                       evt="dbg_xs0")
                em.sig(G, lambda e: e.dma_start(dbg["st0"].ap(), sb["stat"][:]),
                       evt="dbg_st0")
            if DEBUG and tc == 1:
                em.wait(V, "dbg_xn0")
            if DEBUG and tc == 2:
                em.wait(Sq, "dbg_xs0")
            em.wait(T, f"xn_{tc}")
            if tc > 0:
                em.wait(T, f"p1ev_{tc-1}")
            for dc in range(DC):
                em.op(T, lambda e, dc=dc: e.transpose(
                    pbf(dc * 128, dc * 128 + 128), sb["xn"][:, dc * 128:dc * 128 + 128],
                    sb["idf32"][:]))
            em.sig(T, lambda e: e.matmul(pbf(3584, 3592, 0, 64), sb["onesb"][0:64, 0:64],
                                         sb["onesb"][0:64, 0:8], start=True, stop=True),
                   evt=f"tr_{tc}_7")
            if tc > 0:
                em.wait(V, f"mmg_{tc-1}")
            em.wait(V, f"tr_{tc}_7")
            for dc in range(DC):
                em.op(V, lambda e, dc=dc: e.tensor_copy(
                    sb["xnT32"][:, dc * 128:dc * 128 + 128], pbf(dc * 128, dc * 128 + 128)))
                em.sig(V, lambda e, dc=dc, tc=tc: e.tensor_copy(
                    sb["xnTb"][:, dc * 2048 + tc * 128: dc * 2048 + tc * 128 + 128],
                    pbf(dc * 128, dc * 128 + 128)),
                    evt=(f"p1ev_{tc}" if dc == DC - 1 else None))
            em.wait(T, f"p1ev_{tc}")
            if tc == 0:
                pass
            else:
                em.wait(T, f"gmask_{tc-1}")
            for dc in range(DC):
                em.sig(T, lambda e, dc=dc: e.matmul(
                    pbf(1024, 1032), sb["xnT32"][:, dc * 128:dc * 128 + 128],
                    sb["gwT32"][:, dc * 8:dc * 8 + 8], start=(dc == 0), stop=False))
            em.sig(T, lambda e: e.matmul(pbf(1024, 1032), sb["ones32"][:], sb["gbr"][:],
                                         start=False, stop=True), evt=f"mmg_{tc}")
            if tc == 0:
                em.wait(T, "ld_acat")
            else:
                em.wait(T, f"um_{tc-1}")
            for dc in range(DC):
                em.sig(T, lambda e, dc=dc, tc=tc: e.matmul(
                    pbf(1536, 1568),
                    sb["xnTb"][:, dc * 2048 + tc * 128: dc * 2048 + tc * 128 + 128],
                    sb["acatTb"][:, dc * 32:dc * 32 + 32],
                    start=(dc == 0), stop=(dc == DC - 1)),
                    evt=(f"mmu_{tc}" if dc == DC - 1 else None))
            em.wait(V, f"mmg_{tc}")
            if DEBUG and tc == TC - 1:
                em.sig(V, lambda e: e.tensor_copy(sb["dbgl"][:], pbf(1024, 1032)),
                       evt="dbg_glogcp")
            em.sig(V, lambda e: e.reduce_max(sb["stat"][:, 0:1], pbf(1024, 1032),
                                             axis=AX.X), evt=f"gmx_{tc}")
            em.wait(A, f"gmx_{tc}")
            em.sig(A, lambda e: e.copy(sb["stat"][:, 1:2], sb["stat"][:, 0:1]),
                   evt=f"gcp_{tc}")
            em.wait(V, f"gcp_{tc}")
            em.sig(V, lambda e: e.tensor_single_scalar(
                sb["gmask"][:], pbf(1024, 1032), sb["stat"][:, 1:2], ALU.is_ge),
                evt=f"gmask_{tc}")
            em.wait(A, f"gmask_{tc}")
            em.sig(A, lambda e: e.copy(sb["dbgl"][:], sb["gmask"][:]),
                   evt=f"gmcp_{tc}")
            em.wait(V, f"mmu_{tc}")
            if tc > 0:
                em.wait(V, f"mmut_{tc-1}")
            em.wait(V, f"gmcp_{tc}")
            for r in range(4):
                em.sig(V, lambda e, r=r: e.tensor_tensor(
                    bass.AP(umb_h, r, [[32, 128], [4, 8]]),
                    bass.AP(PB, 1536 + r, [[4096, 128], [4, 8]]),
                    sb["dbgl"][:],
                    ALU.mult), evt=(f"um_{tc}" if r == 3 else None))
            em.wait(T, f"um_{tc}")
            if tc > 0:
                em.wait(T, f"uT_{tc-1}")
            em.op(T, lambda e: e.transpose(pbb(4096, 4224, 0, 32), sb["umb"][:],
                                           sb["idbf"][:]))
            em.sig(T, lambda e: e.matmul(pbf(3584, 3592, 0, 64), sb["onesb"][0:64, 0:64],
                                         sb["onesb"][0:64, 0:8], start=True, stop=True),
                   evt=f"mmut_{tc}")
            em.wait(V, f"mmut_{tc}")
            em.sig(V, lambda e, tc=tc: e.tensor_copy(
                sb["uTb"][:, tc * 128:tc * 128 + 128], pbb(4096, 4224, 0, 32)),
                evt=f"uT_{tc}")
            if tc == 0:
                em.wait(T, "ld_wv"); em.wait(T, "ld_bv"); em.wait(T, "v_ones")
            else:
                em.wait(T, f"vev_{tc-1}")
            for nh in range(2):
                for dc in range(DC):
                    em.sig(T, lambda e, nh=nh, dc=dc, tc=tc: e.matmul(
                        pbf(2560 + nh * 512, 3072 + nh * 512),
                        sb["xnTb"][:, dc * 2048 + tc * 128: dc * 2048 + tc * 128 + 128],
                        sb["wvTb"][:, dc * 1024 + nh * 512: dc * 1024 + nh * 512 + 512],
                        start=(dc == 0), stop=False))
                em.sig(T, lambda e, nh=nh: e.matmul(
                    pbf(2560 + nh * 512, 3072 + nh * 512), sb["onesb"][0:1, 0:128],
                    sb["bvr"][:, nh * 512:nh * 512 + 512], start=False, stop=True),
                    evt=f"mmv_{tc}_{nh}")
            em.wait(V, f"mmv_{tc}_1")
            em.sig(V, lambda e, tc=tc: e.tensor_copy(
                bass.AP(vtile_h, tc * 1040, [[16640, 128], [65, 16], [1, 64]]),
                pbf(2560, 3584).rearrange("p (a b) -> p a b", a=16)),
                evt=f"vev_{tc}")

        if PH <= 1:
            em.sig(Sq, lambda e: e.dma_start(
                out_h.ap()[0:128, :],
                sb["xnTb"].bitcast(F32)[:, 0:2048]), evt="dummy_sto")
            em.wait(Sq, "dummy_sto", force=True)
            return
        if DEBUG:
            em.wait(G, "p1ev_15")
            em.wait(G, "rstd_15")
            em.wait(G, "dbg_glogcp")
            em.sig(G, lambda e: e.dma_start(dbg["glog"].ap(), sb["dbgl"][:]),
                   evt="dbg_glog")
            em.sig(G, lambda e: e.dma_start(dbg["stat"].ap(), sb["stat"][:]),
                   evt="dbg_stat")
            em.sig(G, lambda e: e.dma_start(dbg["xnT32"].ap(), sb["xnT32"][:]),
                   evt="dbg_xnT32")
            em.wait(G, "gmask_15")
            em.wait(G, "uT_15")
            em.sig(G, lambda e: e.dma_start(dbg["xnTb2"].ap(), sb["xnTb"][:]),
                   evt="dbg_early1")
            em.sig(G, lambda e: e.dma_start(dbg["gmask2"].ap(), sb["gmask"][:]),
                   evt="dbg_early2")
            em.sig(G, lambda e: e.dma_start(dbg["uT2"].ap(), sb["uTb"][:]),
                   evt="dbg_early3")
        last_mmB = None
        for h in range(NHEAD):
            par = h % 2
            if h == 0:
                em.wait(G, "xn_15")
                em.wait(G, "mmg_15")
            if h >= 2:
                em.wait(G, f"projdone_{h-2}")
            em.sig(G, lambda e, h=h, par=par: e.dma_start(
                sb["wqb"].rearrange("p (x a b) -> p x a b", x=2, a=DC)[:, par],
                bass.AP(dr["wqT"], h * 128, [[2048, 128], [128 * 2048, DC], [1, 128]])),
                evt=f"ldwq_{h}")
            em.sig(G, lambda e, h=h, par=par: e.dma_start(
                sb["wkb"].rearrange("p (x a b) -> p x a b", x=2, a=DC)[:, par],
                bass.AP(dr["wkT"], h * 128, [[2048, 128], [128 * 2048, DC], [1, 128]])),
                evt=f"ldwk_{h}")
            for mat, wb, rot, bc in (("q", "wqb", "qrotb", "bqc"), ("k", "wkb", "krotb", "bkc")):
                em.wait(T, f"ldw{mat}_{h}")
                if h == 0 and mat == "q":
                    em.wait(T, "p1ev_15")
                if h > 0 and mat == "q" and P2SUB == "full":
                    em.wait(T, f"m2s_{h-1}_3")
                    em.wait(T, f"att_{h-1}_3")
                for tt in range(TT):
                    for dc in range(DC):
                        em.sig(T, lambda e, dc=dc, tt=tt, wb=wb, par=par: e.matmul(
                            pbf(0, 512),
                            sb[wb][:, par * 1024 + dc * 128: par * 1024 + dc * 128 + 128],
                            sb["xnTb"][:, dc * 2048 + tt * 512: dc * 2048 + tt * 512 + 512],
                            start=(dc == 0), stop=(dc == DC - 1)),
                            evt=(f"mmA_{h}{mat}{tt}" if dc == DC - 1 else None))
                    em.wait(V, f"mmA_{h}{mat}{tt}")
                    if P2SUB == "projm":
                        em.sig(V, lambda e, rot=rot, tt=tt: e.tensor_copy(
                            sb[rot][:, tt * 512:tt * 512 + 512], pbf(0, 512)),
                            evt=f"evA_{h}{mat}{tt}")
                        em.sig(V, lambda e: None, evt=f"rope_{h}{mat}{tt}")
                        last_mmB = None
                        continue
                    if last_mmB is not None:
                        em.wait(V, last_mmB)
                    em.sig(V, lambda e, bc=bc, h=h: e.tensor_scalar_add(
                        sb["qsb"][:], pbf(0, 512), sb[bc][:, h:h + 1]),
                        evt=f"evA_{h}{mat}{tt}")
                    if P2SUB == "proja":
                        em.sig(V, lambda e, rot=rot, tt=tt: e.tensor_copy(
                            sb[rot][:, tt * 512:tt * 512 + 512], sb["qsb"][:]),
                            evt=f"rope_{h}{mat}{tt}")
                        last_mmB = None
                        continue
                    em.wait(T, f"evA_{h}{mat}{tt}")
                    em.sig(T, lambda e: e.matmul(pbf(512, 1024), sb["psig"][:], sb["qsb"][:],
                                                 start=True, stop=True),
                           evt=f"mmB_{h}{mat}{tt}")
                    last_mmB = f"mmB_{h}{mat}{tt}"
                    if P2SUB == "projb":
                        em.wait(V, f"mmB_{h}{mat}{tt}")
                        em.sig(V, lambda e, rot=rot, tt=tt: e.tensor_copy(
                            sb[rot][:, tt * 512:tt * 512 + 512], pbf(512, 1024)),
                            evt=f"rope_{h}{mat}{tt}")
                        continue
                    if h > 0 and mat == "q" and tt == 0 and P2SUB != "proj":
                        em.wait(V, f"mmS_{h-1}_3_7")
                    em.op(V, lambda e, tt=tt: e.tensor_mul(
                        sb["t1b"][:], sb["qsb"][:], sb["coss"][:, tt * 512:tt * 512 + 512]))
                    em.wait(V, f"mmB_{h}{mat}{tt}")
                    em.op(V, lambda e, tt=tt: e.tensor_mul(
                        sb["t2b"][:], pbf(512, 1024), sb["sins"][:, tt * 512:tt * 512 + 512]))
                    em.sig(V, lambda e, rot=rot, tt=tt: e.tensor_add(
                        sb[rot][:, tt * 512:tt * 512 + 512], sb["t1b"][:], sb["t2b"][:]),
                        evt=f"rope_{h}{mat}{tt}")
            em.alias(f"projdone_{h}", f"mmB_{h}k3")
            if P2SUB == "proj":
                continue
            em.wait(T, f"rope_{h}q3")
            em.wait(T, f"rope_{h}k3")

            def emit_attnv(qt, kcp, kcmax, final):
                kc0 = 2 * kcp
                slot = kcp % 2
                em.wait(T, f"mask_{h}_{qt}_{kcp}")
                if kcp == 0:
                    if qt > 0:
                        em.wait(T, f"vcomb_{h}_{qt-1}")
                    elif h > 0:
                        em.wait(T, f"vcomb_{h-1}_3")
                    else:
                        em.wait(T, "vev_15")
                for mi in range(2):
                    eb = "e1b" if mi == 0 else "e2b"
                    pc = 3072 + mi * 512
                    for j, kc in ((0, kc0), (1, kc0 + 1)):
                        em.sig(T, lambda e, kc=kc, j=j, pc=pc, eb=eb, slot=slot: e.matmul(
                            pbf(pc, pc + 512, 0, 65),
                            sb["vtile"][:, kc * 1040 + h * 65: kc * 1040 + h * 65 + 65],
                            sb[eb][:, slot * 1024 + j * 512: slot * 1024 + j * 512 + 512],
                            start=(kc == 0), stop=(kc == kcmax - 1)),
                            evt=(f"mmPp_{h}_{qt}_{kcp}" if (mi == 1 and j == 1) else None))

            for qt in range(TT):
                kcmax = 4 * qt + 4
                npair = kcmax // 2
                for kcp in range(npair):
                    kc0 = 2 * kcp
                    slot = kcp % 2
                    if kcp > 0:
                        em.wait(T, f"exp_{h}_{qt}_{kcp-1}")
                    elif qt > 0:
                        em.wait(T, f"exp_{h}_{qt-1}_{(4*(qt-1)+4)//2-1}")
                    elif h > 0:
                        em.wait(T, f"exp_{h-1}_3_7")
                    else:
                        em.wait(T, "gmask_15"); em.wait(T, "um_15")
                        em.wait(T, "uT_15"); em.wait(T, "vev_15")
                    for mi in range(2):
                        base = mi * 64
                        scol = 1024 + mi * 1024
                        for j, kc in ((0, kc0), (1, kc0 + 1)):
                            em.sig(T, lambda e, base=base, scol=scol, j=j, kc=kc, qt=qt: e.matmul(
                                pbf(scol + j * 512, scol + j * 512 + 512),
                                sb["krotb"][base:base + 64, kc * 128:kc * 128 + 128],
                                sb["qrotb"][base:base + 64, qt * 512:qt * 512 + 512],
                                start=True, stop=True),
                                evt=(f"mmS_{h}_{qt}_{kcp}" if (mi == 1 and j == 1) else None))
                    em.wait(A, f"mmS_{h}_{qt}_{kcp}")
                    if P2SUB == "scores":
                        em.wait(A, f"mask_{h}_{qt}_{kcp-1}") if kcp > 0 else None
                    elif kcp >= 2:
                        em.wait(A, f"mmPp_{h}_{qt}_{kcp-2}")
                    elif qt > 0:
                        em.wait(A, f"mmPp_{h}_{qt-1}_{(4*(qt-1)+4)//2-1}")
                    elif h > 0:
                        em.wait(A, f"mmPp_{h-1}_3_7")
                    em.sig(A, lambda e, slot=slot: e.activation(
                        sb["e1b"][:, slot * 1024: slot * 1024 + 1024], pbf(1024, 2048),
                        ACTF.Exp, scale=0.125))
                    em.sig(A, lambda e, slot=slot: e.activation(
                        sb["e2b"][:, slot * 1024: slot * 1024 + 1024], pbf(2048, 3072),
                        ACTF.Exp, scale=0.125), evt=f"exp_{h}_{qt}_{kcp}")
                    em.wait(V, f"exp_{h}_{qt}_{kcp}")
                    mops = []
                    for j, kc in ((0, kc0), (1, kc0 + 1)):
                        doff = kc - 4 * qt
                        if doff >= 0:
                            mc = doff * 128
                            for eb in ("e1b", "e2b"):
                                col0 = slot * 1024 + j * 512
                                if mc > 0:
                                    mops.append(("ms", eb, col0, mc))
                                mops.append(("tm", eb, col0, mc))
                    for i, (kind, eb, col0, mc) in enumerate(mops):
                        last = (i == len(mops) - 1)
                        ev = f"mask_{h}_{qt}_{kcp}" if last else None
                        if kind == "ms":
                            em.sig(V, lambda e, eb=eb, col0=col0, mc=mc: e.tensor_copy(
                                sb[eb][:, col0: col0 + mc], sb["zerob"][:, 0:mc]), evt=ev)
                        else:
                            em.sig(V, lambda e, eb=eb, col0=col0, mc=mc: e.tensor_mul(
                                sb[eb][:, col0 + mc: col0 + mc + 128],
                                sb[eb][:, col0 + mc: col0 + mc + 128],
                                sb["trim"][:]), evt=ev)
                    if not mops:
                        em.alias(f"mask_{h}_{qt}_{kcp}", f"exp_{h}_{qt}_{kcp}")
                    if P2SUB != "scores" and kcp > 0:
                        emit_attnv(qt, kcp - 1, kcmax, False)
                if P2SUB == "scores":
                    continue
                emit_attnv(qt, npair - 1, kcmax, True)
                lastp = npair - 1

                em.wait(V, f"mmPp_{h}_{qt}_{lastp}")
                prev = (h, qt - 1) if qt > 0 else (h - 1, 3)
                if prev[0] >= 0:
                    em.wait(V, f"bcRS_{prev[0]}_{prev[1]}")
                    em.wait(V, f"rs_{prev[0]}_{prev[1]}")
                em.op(V, lambda e: e.tensor_copy(sb["rbf"][64:65, 0:512],
                                                 pbf(3584, 4096, 64, 65)))
                em.op(V, lambda e: e.scalar_tensor_tensor(
                    sb["rtmp"][64:65, 0:512], pbf(3072, 3584, 64, 65), 1e-3,
                    sb["rbf"][64:65, 0:512], ALU.mult, ALU.mult))
                em.wait(V, "lam_bc")
                em.op(V, lambda e, h=h: e.tensor_scalar_mul(
                    sb["rbf"][64:65, 512:1024], pbf(3072, 3584, 64, 65),
                    sb["lbc"][64:65, h:h + 1]))
                em.sig(V, lambda e: e.tensor_mul(
                    sb["rtmp"][64:65, 512:1024], sb["rtmp"][64:65, 0:512],
                    sb["rtmp"][64:65, 0:512]), evt=f"rows_{h}_{qt}")
                em.wait(T, f"rows_{h}_{qt}")
                if prev[0] >= 0:
                    em.wait(T, f"m2s_{prev[0]}_{prev[1]}")
                    em.wait(T, f"att_{prev[0]}_{prev[1]}")
                em.sig(T, lambda e: e.matmul(pbf(0, 512, 0, 64), sb["onesb"][64:65, 0:64],
                                             sb["rbf"][64:65, 0:512], start=True, stop=True),
                       evt=f"bcA_{h}_{qt}")
                em.sig(T, lambda e: e.matmul(pbf(512, 1024, 0, 64), sb["onesb"][64:65, 0:64],
                                             sb["rbf"][64:65, 512:1024], start=True, stop=True),
                       evt=f"bcB_{h}_{qt}")
                em.wait(V, f"bcB_{h}_{qt}")
                em.op(V, lambda e: e.tensor_copy(sb["ct"][0:64, :], pbf(0, 512, 0, 64)))
                em.op(V, lambda e: e.tensor_mul(sb["cv"][0:64, :], pbf(3072, 3584, 0, 64),
                                                sb["ct"][0:64, :]))
                em.op(V, lambda e: e.tensor_copy(sb["ct"][0:64, :], pbf(512, 1024, 0, 64)))
                em.op(V, lambda e: e.tensor_mul(sb["m2s"][0:64, :], pbf(3584, 4096, 0, 64),
                                                sb["ct"][0:64, :]))
                em.sig(V, lambda e: e.tensor_sub(sb["cv"][0:64, :], sb["cv"][0:64, :],
                                                 sb["m2s"][0:64, :]), evt=f"vcomb_{h}_{qt}")
                em.sig(V, lambda e: e.tensor_mul(sb["v2b"][0:64, :], sb["cv"][0:64, :],
                                                 sb["cv"][0:64, :]), evt=f"v2_{h}_{qt}")
                em.wait(T, f"v2_{h}_{qt}")
                em.sig(T, lambda e: e.matmul(pbf(0, 512, 0, 64), sb["onesb"][0:64, 0:64],
                                             sb["v2b"][0:64, :], start=True, stop=True),
                       evt=f"mmM2_{h}_{qt}")
                em.wait(V, f"mmM2_{h}_{qt}")
                em.sig(V, lambda e: e.tensor_copy(sb["m2s"][64:65, :], pbf(0, 512, 0, 1)),
                       evt=f"m2s_{h}_{qt}")
                em.sig(V, lambda e: e.scalar_tensor_tensor(
                    sb["zt"][64:65, 0:512], sb["m2s"][64:65, :], 1.0 / 64,
                    sb["rtmp"][64:65, 512:1024], ALU.mult, ALU.add), evt=f"z_{h}_{qt}")
                em.wait(A, f"z_{h}_{qt}")
                em.op(A, lambda e: e.activation(sb["zt"][64:65, 512:1024],
                                                sb["zt"][64:65, 0:512], ACTF.Ln))
                em.sig(A, lambda e: e.activation(sb["rsb"][64:65, :],
                                                 sb["zt"][64:65, 512:1024], ACTF.Exp,
                                                 scale=-0.5), evt=f"rs_{h}_{qt}")
                em.wait(T, f"rs_{h}_{qt}")
                em.sig(T, lambda e: e.matmul(pbf(512, 1024, 0, 64), sb["onesb"][64:65, 0:64],
                                             sb["rsb"][64:65, 0:512], start=True, stop=True),
                       evt=f"bcRS_{h}_{qt}")
                em.wait(V, f"bcRS_{h}_{qt}")
                hc, hodd = h // 2, h % 2
                acol = hc * 2048 + qt * 512
                if hodd == 0:
                    em.sig(V, lambda e, acol=acol: e.tensor_mul(
                        sb["attoT"][0:64, acol:acol + 512], sb["cv"][0:64, :],
                        pbf(512, 1024, 0, 64)), evt=f"att_{h}_{qt}")
                else:
                    em.op(V, lambda e: e.tensor_mul(
                        sb["atmp"][0:64, :], sb["cv"][0:64, :], pbf(512, 1024, 0, 64)))
                    em.sig(V, lambda e, acol=acol: e.tensor_copy(
                        sb["attoT"][64:128, acol:acol + 512], sb["atmp"][0:64, :]),
                        evt=f"att_{h}_{qt}")

        if PH <= 2:
            em.sig(Sq, lambda e: e.dma_start(
                out_h.ap()[0:128, :],
                sb["attoT"].bitcast(F32)[:, 0:2048]), evt="dummy_sto")
            em.wait(Sq, "dummy_sto", force=True)
            return
        for dc in range(DC):
            par = dc % 2
            if dc >= 2:
                em.wait(G, f"mmST_{dc-2}_3")
            em.sig(G, lambda e, dc=dc, par=par: e.dma_start(
                sb["wost"].rearrange("p (x a b) -> p x a b", x=2, a=DC)[:, par],
                bass.AP(dr["woT"], dc * 128, [[D, 128], [128 * D, DC], [1, 128]])),
                evt=f"ldwo_{dc}")
            em.wait(V, f"ldwo_{dc}")
            if dc >= 2:
                em.wait(V, f"mmST_{dc-2}_3")
            for dcin in range(DC):
                em.sig(V, lambda e, dcin=dcin, par=par: e.tensor_scalar_mul(
                    sb["wost"][:, par * 1024 + dcin * 128: par * 1024 + dcin * 128 + 128],
                    sb["wost"][:, par * 1024 + dcin * 128: par * 1024 + dcin * 128 + 128],
                    sb["hnwc"][:, dcin:dcin + 1]),
                    evt=(f"wosc_{dc}" if dcin == DC - 1 else None))
            for tt in range(TT):
                idx = dc * TT + tt
                ppar = idx % 2
                em.wait(T, f"att_{H-1}_3")
                em.wait(T, f"exp_{H-1}_3_7")
                em.wait(T, f"wosc_{dc}")
                if idx >= 2:
                    pdc, ptt = (idx - 2) // TT, (idx - 2) % TT
                    em.wait(T, f"sev_{pdc}_{ptt}")
                for dcin in range(DC):
                    em.sig(T, lambda e, dcin=dcin, tt=tt, par=par, ppar=ppar: e.matmul(
                        pbf(1024 + ppar * 512, 1536 + ppar * 512),
                        sb["wost"][:, par * 1024 + dcin * 128: par * 1024 + dcin * 128 + 128],
                        sb["attoT"][:, dcin * 2048 + tt * 512: dcin * 2048 + tt * 512 + 512],
                        start=(dcin == 0), stop=(dcin == DC - 1)),
                        evt=(f"mmST_{dc}_{tt}" if dcin == DC - 1 else None))
                em.wait(V, f"mmST_{dc}_{tt}")
                em.sig(V, lambda e, dc=dc, tt=tt, ppar=ppar: e.tensor_scalar_add(
                    sb["sTb"][:, dc * 2048 + tt * 512: dc * 2048 + tt * 512 + 512],
                    pbf(1024 + ppar * 512, 1536 + ppar * 512), sb["boc"][:, dc:dc + 1]),
                    evt=f"sev_{dc}_{tt}")

        if PH <= 3:
            em.sig(Sq, lambda e: e.dma_start(
                out_h.ap()[0:128, :],
                sb["sTb"].bitcast(F32)[:, 0:2048]), evt="dummy_sto")
            em.wait(Sq, "dummy_sto", force=True)
            return
        for tt in range(TT):
            for fc in range(FC):
                idx = tt * FC + fc
                par = idx % 2
                if idx == 0:
                    em.wait(G, f"mmST_{DC-1}_3")
                    em.wait(G, f"att_{H-1}_3")
                    em.wait(G, f"rs_{H-1}_3")
                if idx >= 2:
                    em.wait(G, f"mmI_{idx-2}")
                em.sig(G, lambda e, fc=fc, par=par: e.dma_start(
                    sb["wib"].rearrange("p (x a b) -> p x a b", x=2, a=DC)[:, par],
                    bass.AP(dr["wiT"], fc * 128, [[F, 128], [128 * F, DC], [1, 128]])),
                    evt=f"ldwi_{idx}")
                em.wait(T, f"ldwi_{idx}")
                if idx == 0:
                    em.wait(T, "ld_bct")
                    em.wait(T, "uT_15")
                    em.wait(T, f"m2s_{H-1}_3")
                    em.wait(T, f"att_{H-1}_3")
                if idx >= 2:
                    em.wait(T, f"iev_{idx-2}")
                for dc in range(DC):
                    em.sig(T, lambda e, dc=dc, tt=tt, par=par: e.matmul(
                        pbf(par * 512, par * 512 + 512),
                        sb["wib"][:, par * 1024 + dc * 128: par * 1024 + dc * 128 + 128],
                        sb["xnTb"][:, dc * 2048 + tt * 512: dc * 2048 + tt * 512 + 512],
                        start=(dc == 0), stop=False))
                em.sig(T, lambda e, fc=fc, tt=tt, par=par: e.matmul(
                    pbf(par * 512, par * 512 + 512),
                    sb["bctb"][0:32, fc * 128:fc * 128 + 128],
                    sb["uTb"][0:32, tt * 512:tt * 512 + 512],
                    start=False, stop=True), evt=f"mmI_{idx}")
                em.wait(V, f"mmI_{idx}")
                if idx == 0:
                    em.wait(V, f"mmPp_{H-1}_3_7")
                if tt > 0:
                    em.wait(V, f"mmG_{(tt-1)*DC + DC - 1}")
                em.sig(V, lambda e, fc=fc, par=par: e.tensor_scalar(
                    sb["interTb"][:, fc * 512:fc * 512 + 512],
                    pbf(par * 512, par * 512 + 512),
                    sb["wibc"][:, fc:fc + 1], 0.0, ALU.add, ALU.max), evt=f"iev_{idx}")
            for dc in range(DC):
                gidx = tt * DC + dc
                par = gidx % 2
                if gidx >= 2:
                    em.wait(G, f"mmG_{gidx-2}")
                em.sig(G, lambda e, dc=dc, par=par: e.dma_start(
                    sb["fwoTb"].rearrange("p (x a b) -> p x a b", x=2, a=FC)[:, par],
                    bass.AP(dr["fwoT"], dc * 128, [[D, 128], [128 * D, FC], [1, 128]])),
                    evt=f"ldfw_{gidx}")
                if gidx == 0:
                    em.wait(Sq, f"z_{H-1}_3")
                if gidx >= 2:
                    em.wait(Sq, f"oev_{gidx-2}")
                em.sig(Sq, lambda e, dc=dc, tt=tt, par=par: e.dma_start(
                    sb["xTs"][:, par * 512:(par + 1) * 512],
                    dr["xT"].ap()[dc * 128:(dc + 1) * 128, tt * 512:(tt + 1) * 512]),
                    evt=f"ldxT_{gidx}")
                em.wait(T, f"ldfw_{gidx}")
                em.wait(T, f"iev_{tt * FC + FC - 1}")
                if gidx >= 2:
                    em.wait(T, f"gev_{gidx-2}")
                for fc in range(FC):
                    em.sig(T, lambda e, fc=fc, par=par: e.matmul(
                        pbf(2048 + par * 512, 2560 + par * 512),
                        sb["fwoTb"][:, par * 4096 + fc * 128: par * 4096 + fc * 128 + 128],
                        sb["interTb"][:, fc * 512:fc * 512 + 512],
                        start=(fc == 0), stop=(fc == FC - 1)),
                        evt=(f"mmG_{gidx}" if fc == FC - 1 else None))
                em.wait(A, f"mmG_{gidx}")
                if gidx >= 2:
                    em.wait(A, f"oev_{gidx-2}")
                em.sig(A, lambda e, dc=dc, par=par: e.activation(
                    sb["gbuf"][:, par * 512:(par + 1) * 512],
                    pbf(2048 + par * 512, 2560 + par * 512), ACTF.Silu,
                    bias=sb["wobc"][:, dc:dc + 1]), evt=f"gev_{gidx}")
                em.wait(V, f"gev_{gidx}")
                em.wait(V, f"ldxT_{gidx}")
                if gidx == 0:
                    em.wait(V, f"bcB_{H-1}_3")
                if gidx >= 2:
                    em.wait(V, f"sto_{gidx-2}")
                em.op(V, lambda e, dc=dc, tt=tt, par=par: e.tensor_mul(
                    sb["tbuf"][:], sb["sTb"][:, dc * 2048 + tt * 512: dc * 2048 + tt * 512 + 512],
                    sb["gbuf"][:, par * 512:(par + 1) * 512]))
                em.sig(V, lambda e, par=par: e.tensor_add(
                    sb["outb"][:, par * 512:(par + 1) * 512], sb["tbuf"][:],
                    sb["xTs"][:, par * 512:(par + 1) * 512]), evt=f"oev_{gidx}")
                em.wait(Sq, f"oev_{gidx}")
                em.sig(Sq, lambda e, dc=dc, tt=tt, par=par: e.dma_start(
                    out_h.ap()[dc * 128:(dc + 1) * 128, tt * 512:(tt + 1) * 512],
                    sb["outb"][:, par * 512:(par + 1) * 512]), evt=f"sto_{gidx}")

        if DEBUG:
            em.wait(G, f"sto_{TT * DC - 1}")
            dbg_list = [("xnTb", "xnTb"), ("v", "vtile"), ("uT", "uTb"),
                        ("qrot", "qrotb"), ("krot", "krotb"), ("att", "attoT"),
                        ("sT", "sTb"), ("inter", "interTb"), ("gmask", "gmask")]
            for nm, src in dbg_list:
                em.sig(G, lambda e, nm=nm, src=src: e.dma_start(
                    dbg[nm].ap(), sb[src][:]), evt=f"dbg_{nm}")
            em.wait(G, f"dbg_{dbg_list[-1][0]}", force=True)
        for gidx in range(TT * DC - NDMASEM, TT * DC):
            em.wait(Sq, f"sto_{gidx}", force=True)

    with es:
        block = es.enter_context(nc.Block())
        prog(Em("count", sems, events))

        @block.tensor
        def _(eng):
            prog(Em("T", sems, events, eng))

        @block.vector
        def _(eng):
            prog(Em("V", sems, events, eng))

        @block.scalar
        def _(eng):
            prog(Em("A", sems, events, eng))

        @block.gpsimd
        def _(eng):
            prog(Em("G", sems, events, eng))

        @block.sync
        def _(eng):
            prog(Em("S", sems, events, eng))

    return nc


_NC_CACHE = None


def kernel(**inputs):
    global _NC_CACHE
    consts = _host_consts()
    w = _prep_weights(inputs)
    x = np.asarray(inputs["L0_attn_x"], np.float32)
    if _NC_CACHE is None:
        _NC_CACHE = build_nc()
    nc = _NC_CACHE
    in_maps = []
    for c in range(NCORE):
        m = {}
        m.update(consts)
        m.update(w)
        xc = np.ascontiguousarray(x[c])
        m["x"] = xc
        m["xT"] = np.ascontiguousarray(xc.T)
        in_maps.append(m)
    res = run_bass_kernel_spmd(nc, in_maps, core_ids=list(range(NCORE)))
    out = np.stack([np.ascontiguousarray(r["outT"].T) for r in res.results], axis=0)
    return out


# version 02
# speedup vs baseline: 1.0069x; 1.0069x over previous
import numpy as np
import ml_dtypes
import concourse.bass as bass
import concourse.mybir as mybir
from concourse.bass_utils import run_bass_kernel_spmd
from contextlib import ExitStack

F32 = mybir.dt.float32
BF16 = mybir.dt.bfloat16
ALU = mybir.AluOpType
ACTF = mybir.ActivationFunctionType
AX = mybir.AxisListType
BF = ml_dtypes.bfloat16

S, D, H = 2048, 1024, 16
TC, TT, DC, FC = 16, 4, 8, 32
F = 4096
NCORE = 8
DEBUG = False
PH = 4
P2SUB = 'full'
NHEAD = H


def _host_consts():
    j = np.arange(32)
    inv = 10000.0 ** (-(j.astype(np.float64)) / 32.0)
    t = np.arange(S, dtype=np.float64)
    ang = t[None, :] * inv[:, None]
    p = np.arange(128)
    coss = np.cos(ang)[p % 32, :]
    sins = np.sin(ang)[p % 32, :] * np.where((p % 64) < 32, -1.0, 1.0)[:, None]
    psig = np.zeros((128, 128))
    psig[p, p ^ 32] = 1.0
    trim = (np.arange(128)[None, :] >= np.arange(128)[:, None]).astype(np.float64)
    return {
        "coss": coss.astype(np.float32).astype(BF),
        "sins": sins.astype(np.float32).astype(BF),
        "psig": psig.astype(BF),
        "trim": trim.astype(BF),
        "onesb": np.ones((128, 128), BF),
        "ones32": np.ones((1, 128), np.float32),
        "idf32": np.eye(128, dtype=np.float32),
        "idbf": np.eye(128, dtype=np.float32).astype(BF),
        "eps5": np.full((128, 1), 1e-5, np.float32),
        "zerob": np.zeros((128, 384), BF),
    }


def _prep_weights(I):
    g = lambda k: np.asarray(I[k], np.float32)
    w = {}
    w["wqT"] = g("L0_attn_Wq").T
    w["wkT"] = g("L0_attn_Wk").T
    w["wvT"] = g("L0_attn_Wv").T
    w["woT"] = g("L0_attn_Wo").T
    w["wiT"] = g("L0_ffn_wi_W").T
    w["fwoT"] = g("L0_ffn_wo_W").T
    w["gwT"] = g("L0_ffn_gate_W").T
    w["acatT"] = g("L0_ffn_lora_A").reshape(32, 1024).T
    w["bcatT"] = g("L0_ffn_lora_B").transpose(1, 0, 2).reshape(4096, 32).T
    w["bq_c"] = g("L0_attn_bq").reshape(16, 128).T
    w["bk_c"] = g("L0_attn_bk").reshape(16, 128).T
    w["bv_r"] = g("L0_attn_bv").reshape(1, 1024)
    w["bo_c"] = g("L0_attn_bo").reshape(8, 128).T
    w["wib_c"] = g("L0_ffn_wi_b").reshape(32, 128).T
    w["wob_c"] = g("L0_ffn_wo_b").reshape(8, 128).T
    w["gb_r"] = g("L0_ffn_gate_b").reshape(1, 8)
    w["hnw_c"] = g("L0_attn_head_norm_w").reshape(1024).reshape(8, 128).T
    w["lamA"] = np.concatenate([g("L0_attn_lq1"), g("L0_attn_lq2")], axis=1)
    w["lamB"] = np.concatenate([g("L0_attn_lk1"), g("L0_attn_lk2")], axis=1)
    return {k: np.ascontiguousarray(v) for k, v in w.items()}


NDMASEM = 8


class Em:
    CSEM = {"T": "tT", "V": "tV", "A": "tA"}

    def __init__(self, mode, sems, events, eng=None):
        self.mode = mode
        self.sems = sems
        self.events = events
        self.eng = eng
        self.counts = {}
        self.qidx = {"S": 0, "G": 0}
        self.waited = {}
        self.own_sems = set()
        if mode in self.CSEM:
            self.own_sems.add(self.CSEM[mode])
        elif mode in ("S", "G"):
            self.own_sems.update(f"t{mode}{i}" for i in range(NDMASEM))

    def op(self, engine, fn):
        if self.mode == engine:
            fn(self.eng)

    def sig(self, engine, fn, evt=None):
        if engine in ("S", "G"):
            slot = self.qidx[engine] % NDMASEM
            self.qidx[engine] += 1
            sem = f"t{engine}{slot}"
            prev = self.counts.get(sem, 0)
            if self.mode == "count":
                self.counts[sem] = prev + 16
                if evt is not None:
                    assert evt not in self.events, f"dup evt {evt}"
                    self.events[evt] = (sem, prev + 16)
            elif self.mode == engine:
                if prev > 0:
                    self.eng.wait_ge(self.sems[sem], prev)
                    self.waited[sem] = max(self.waited.get(sem, -1), prev)
                self.counts[sem] = prev + 16
                fn(self.eng).then_inc(self.sems[sem], 16)
            else:
                self.counts[sem] = prev + 16
            return
        sem = self.CSEM[engine]
        if self.mode == "count":
            self.counts[sem] = self.counts.get(sem, 0) + 1
            if evt is not None:
                assert evt not in self.events, f"dup evt {evt}"
                self.events[evt] = (sem, self.counts[sem])
        elif self.mode == engine:
            ins = fn(self.eng)
            assert ins is not None, "sig() requires a real instruction (nop+inc faults HW)"
            ins.then_inc(self.sems[sem], 1)

    def alias(self, evt_new, evt_old):
        if self.mode == "count":
            assert evt_new not in self.events and evt_old in self.events
            self.events[evt_new] = self.events[evt_old]

    def wait(self, engine, evt, force=False):
        if self.mode == "count":
            assert evt in self.events, f"wait on future evt {evt}"
            return
        if self.mode == engine:
            sem, val = self.events[evt]
            if (not force) and sem in self.own_sems and sem in self.CSEM.values():
                return
            if self.waited.get(sem, -1) >= val:
                return
            self.waited[sem] = val
            self.eng.wait_ge(self.sems[sem], val)


def build_nc():
    nc = bass.Bass("TRN2", target_bir_lowering=False, detect_race_conditions=False)
    dr = {}

    def din(name, shape, dt=F32):
        dr[name] = nc.dram_tensor(name, list(shape), dt, kind="ExternalInput")

    din("x", (S, D)); din("xT", (D, S))
    din("wqT", (D, 2 * D)); din("wkT", (D, 2 * D)); din("wvT", (D, D)); din("woT", (D, D))
    din("wiT", (D, F)); din("fwoT", (F, D)); din("gwT", (D, 8))
    din("acatT", (D, 32)); din("bcatT", (32, F))
    din("bq_c", (128, 16)); din("bk_c", (128, 16)); din("bv_r", (1, D))
    din("bo_c", (128, 8)); din("wib_c", (128, 32)); din("wob_c", (128, 8))
    din("gb_r", (1, 8)); din("hnw_c", (128, 8))
    din("lamA", (16, 128)); din("lamB", (16, 128)); din("eps5", (128, 1))
    din("coss", (128, S), BF16); din("sins", (128, S), BF16)
    din("psig", (128, 128), BF16); din("trim", (128, 128), BF16)
    din("onesb", (128, 128), BF16); din("zerob", (128, 384), BF16); din("ones32", (1, 128)); din("idf32", (128, 128))
    din("idbf", (128, 128), BF16)
    out_h = nc.dram_tensor("outT", [D, S], F32, kind="ExternalOutput")
    dbg = {}
    if DEBUG:
        for nm, shp in [("xnTb", (128, 16384)), ("v", (128, 16640)),
                        ("uT", (32, 2048)), ("qrot", (128, 2048)),
                        ("krot", (128, 2048)), ("att", (128, 16384)),
                        ("sT", (128, 16384)), ("inter", (128, 16384)),
                        ("gmask", (128, 8)), ("xnTb2", (128, 16384)),
                        ("gmask2", (128, 8)), ("uT2", (32, 2048)),
                        ("glog", (128, 8)), ("stat", (128, 8)),
                        ("xnT32", (128, 1024)), ("xn0", (128, 1024)),
                        ("xs0", (128, 1024)), ("st0", (128, 8))]:
            dbg[nm] = nc.dram_tensor("d_" + nm, list(shp), F32, kind="ExternalOutput")

    es = ExitStack()
    sb = {}

    def tile(name, shape, dt):
        h = es.enter_context(nc.sbuf_tensor("t_" + name, list(shape), dt))
        sb[name] = h.ap()
        return h

    tile("xnTb", (128, 16384), BF16)
    tile("attoT", (128, 16384), BF16)
    tile("sTb", (128, 16384), BF16)
    tile("uTb", (32, 2048), BF16)
    vtile_h = tile("vtile", (128, 16640), BF16)
    tile("coss", (128, S), BF16); tile("sins", (128, S), BF16)
    tile("psig", (128, 128), BF16); tile("trim", (128, 128), BF16)
    tile("onesb", (128, 128), BF16); tile("zerob", (128, 384), BF16); tile("ones32", (1, 128), F32)
    tile("idf32", (128, 128), F32); tile("idbf", (128, 128), BF16)
    tile("bqc", (128, 16), F32); tile("bkc", (128, 16), F32)
    tile("bvr", (1, D), BF16); tile("boc", (128, 8), F32)
    tile("wibc", (128, 32), F32); tile("wobc", (128, 8), F32)
    tile("gbr", (1, 8), F32); tile("hnwc", (128, 8), F32)
    tile("lamA", (16, 128), F32); tile("lamB", (16, 128), F32)
    tile("lprod", (16, 128), F32); tile("lsc", (16, 8), F32)
    tile("lbc", (128, 16), F32)
    tile("gwT32", (128, 64), F32)
    tile("acatTb", (128, 256), BF16)
    tile("bctb", (32, F), BF16)
    tile("stat", (128, 8), F32)
    tile("eps5", (128, 1), F32)
    tile("gmask", (128, 8), F32)
    tile("dbgl", (128, 8), F32)
    umb_h = tile("umb", (128, 32), BF16)

    ARN = es.enter_context(nc.sbuf_tensor("ARN", [128, 12800], F32))
    ARNf = ARN.ap()
    ARNb = ARN.ap().bitcast(BF16)

    def af(byte_off, cols):
        assert byte_off % 4 == 0
        return ARNf[:, byte_off // 4: byte_off // 4 + cols]

    def ab(byte_off, cols):
        assert byte_off % 2 == 0
        return ARNb[:, byte_off // 2: byte_off // 2 + cols]

    sb["xst"] = af(0, 2048)
    sb["xsq"] = af(8192, 1024)
    sb["xn"] = af(12288, 1024)
    sb["xnT32"] = af(16384, 1024)
    sb["wvTb"] = ab(20480, 8192)
    sb["wqb"] = ab(0, 2048)
    sb["wkb"] = ab(4096, 2048)
    sb["qsb"] = ab(8192, 512)
    sb["t1b"] = ab(9216, 512)
    sb["t2b"] = ab(10240, 512)
    sb["qrotb"] = ab(11264, 2048)
    sb["krotb"] = ab(15360, 2048)
    sb["e1b"] = ab(19456, 2048)
    sb["e2b"] = ab(23552, 2048)
    sb["rtmp"] = af(27648, 1024)
    sb["rbf"] = ab(31744, 1024)
    sb["ct"] = af(33792, 512)
    sb["cv"] = af(35840, 512)
    sb["v2b"] = ab(37888, 512)
    sb["zt"] = af(38912, 1024)
    sb["m2s"] = af(43008, 512)
    sb["rsb"] = ab(45056, 512)
    sb["atmp"] = ab(46080, 512)
    sb["wost"] = ab(47104, 2048)
    sb["interTb"] = vtile_h.ap()[:, 0:16384]
    sb["wib"] = ab(0, 2048)
    sb["fwoTb"] = ab(4096, 8192)
    sb["gbuf"] = af(20480, 1024)
    sb["tbuf"] = af(24576, 512)
    sb["xTs"] = af(26624, 1024)
    sb["outb"] = af(30720, 1024)

    PB = es.enter_context(nc.psum_tensor("PB", [128, 4096], F32))

    def pbf(c0, c1, p0=0, p1=128):
        return PB[p0:p1, c0:c1]

    def pbb(c0b, c1b, p0=0, p1=128):
        return PB.ap().bitcast(BF16)[p0:p1, c0b:c1b]

    sems = {}
    semnames = ["tT", "tV", "tA"] + [f"tS{i}" for i in range(NDMASEM)] \
        + [f"tG{i}" for i in range(NDMASEM)]
    for sname in semnames:
        sems[sname] = es.enter_context(nc.semaphore(sname))
    events = {}

    def prog(em: Em):
        V, T, A, G, Sq = "V", "T", "A", "G", "S"

        small = [("coss", "coss"), ("sins", "sins"), ("psig", "psig"),
                 ("trim", "trim"), ("onesb", "onesb"), ("ones32", "ones32"),
                 ("idf32", "idf32"), ("idbf", "idbf"),
                 ("bq_c", "bqc"), ("bk_c", "bkc"), ("bo_c", "boc"),
                 ("wib_c", "wibc"), ("wob_c", "wobc"), ("gb_r", "gbr"),
                 ("hnw_c", "hnwc"), ("lamA", "lamA"), ("lamB", "lamB"),
                 ("eps5", "eps5"), ("zerob", "zerob")]
        for i, (nm, dst) in enumerate(small):
            em.sig(Sq, lambda e, nm=nm, dst=dst: e.dma_start(sb[dst][:], dr[nm].ap()),
                   evt=f"lds_{i}")
        em.sig(Sq, lambda e: e.dma_start(
            sb["gwT32"].rearrange("p (a b) -> p a b", a=DC),
            bass.AP(dr["gwT"], 0, [[8, 128], [128 * 8, DC], [1, 8]])), evt="ld_gw")

        def ld_barrier(engine):
            for i in range(len(small) + 1 - NDMASEM, len(small)):
                em.wait(engine, f"lds_{i}")
            em.wait(engine, "ld_gw")
        em.sig(G, lambda e: e.dma_start(
            sb["wvTb"].rearrange("p (a b) -> p a b", a=DC),
            bass.AP(dr["wvT"], 0, [[D, 128], [128 * D, DC], [1, D]])), evt="ld_wv")
        em.sig(G, lambda e: e.dma_start(
            sb["acatTb"].rearrange("p (a b) -> p a b", a=DC),
            bass.AP(dr["acatT"], 0, [[32, 128], [128 * 32, DC], [1, 32]])), evt="ld_acat")
        em.sig(G, lambda e: e.dma_start(sb["bvr"][:], dr["bv_r"].ap()), evt="ld_bv")
        em.sig(G, lambda e: e.dma_start(sb["bctb"][:], dr["bcatT"].ap()), evt="ld_bct")

        ld_barrier("A")

        ld_barrier("V")
        em.op(V, lambda e: e.tensor_scalar_mul(sb["hnwc"][:], sb["hnwc"][:], 0.8))
        em.sig(V, lambda e: e.memset(vtile_h[:], 1.0), evt="v_ones")
        em.op(V, lambda e: e.tensor_mul(sb["lprod"][:], sb["lamA"][:], sb["lamB"][:]))
        em.op(V, lambda e: e.reduce_sum(sb["lsc"][:, 0:1], sb["lprod"][:, 0:64], axis=AX.X))
        em.sig(V, lambda e: e.reduce_sum(sb["lsc"][:, 1:2], sb["lprod"][:, 64:128], axis=AX.X),
               evt="lam_s")
        em.wait(A, "lam_s")
        em.sig(A, lambda e: e.activation(sb["lsc"][:, 2:4], sb["lsc"][:, 0:2], ACTF.Exp),
               evt="lam_e")
        em.wait(V, "lam_e")
        em.sig(V, lambda e: e.scalar_tensor_tensor(
            sb["lsc"][:, 5:6], sb["lsc"][:, 2:3], 0.2, sb["lsc"][:, 3:4],
            ALU.add, ALU.subtract), evt="lam_c")
        ld_barrier("T")
        em.wait(Sq, "lam_c")
        em.sig(Sq, lambda e: e.dma_start(sb["lbc"][64:65, 0:16], sb["lsc"][:, 5:6]),
               evt="lam_bc")

        if PH <= 0:
            em.sig(Sq, lambda e: e.dma_start(
                out_h.ap()[0:128, :],
                sb["xnTb"].bitcast(F32)[:, 0:2048]), evt="dummy_sto")
            em.wait(Sq, "dummy_sto", force=True)
            return
        for tc in range(TC):
            par = tc % 2
            xs = sb["xst"][:, par * D:(par + 1) * D]
            if tc >= 2:
                em.wait(Sq, f"xn_{tc-2}")
            em.sig(Sq, lambda e, tc=tc, xs=xs: e.dma_start(
                xs, dr["x"].ap()[tc * 128:(tc + 1) * 128, :]), evt=f"ldx_{tc}")
            em.wait(V, f"ldx_{tc}")
            st = sb["stat"]
            if tc > 0:
                em.wait(V, f"rstd_{tc-1}")
            em.op(V, lambda e, xs=xs: e.reduce_sum(st[:, 0:1], xs, axis=AX.X))
            em.op(V, lambda e, xs=xs: e.tensor_mul(sb["xsq"][:], xs, xs))
            em.op(V, lambda e: e.tensor_mul(st[:, 2:3], st[:, 0:1], st[:, 0:1]))
            em.op(V, lambda e: e.reduce_sum(st[:, 1:2], sb["xsq"][:], axis=AX.X))
            em.op(V, lambda e: e.tensor_scalar_mul(st[:, 3:4], st[:, 0:1], 1.0 / D))
            em.sig(V, lambda e: e.tensor_scalar(
                st[:, 5:6], st[:, 2:3], -1.0 / (D * D), sb["eps5"][:, 0:1],
                ALU.mult, ALU.add), evt=f"var_{tc}")
            em.wait(A, f"var_{tc}")
            em.op(A, lambda e: e.activation(st[:, 6:7], st[:, 1:2], ACTF.Ln,
                                            scale=1.0 / D, bias=st[:, 5:6]))
            em.sig(A, lambda e: e.activation(st[:, 7:8], st[:, 6:7], ACTF.Exp, scale=-0.5),
                   evt=f"rstd_{tc}")
            em.wait(V, f"rstd_{tc}")
            if tc > 0:
                em.wait(V, f"tr_{tc-1}_7")
            em.sig(V, lambda e, xs=xs: e.tensor_scalar(
                sb["xn"][:], xs, st[:, 3:4], st[:, 7:8], ALU.subtract, ALU.mult),
                evt=f"xn_{tc}")
            if DEBUG and tc == 0:
                em.wait(G, "xn_0")
                em.sig(G, lambda e: e.dma_start(dbg["xn0"].ap(), sb["xn"][:]),
                       evt="dbg_xn0")
                em.sig(G, lambda e: e.dma_start(dbg["xs0"].ap(), sb["xst"][:, 0:1024]),
                       evt="dbg_xs0")
                em.sig(G, lambda e: e.dma_start(dbg["st0"].ap(), sb["stat"][:]),
                       evt="dbg_st0")
            if DEBUG and tc == 1:
                em.wait(V, "dbg_xn0")
            if DEBUG and tc == 2:
                em.wait(Sq, "dbg_xs0")
            em.wait(T, f"xn_{tc}")
            if tc > 0:
                em.wait(T, f"p1ev_{tc-1}")
            for dc in range(DC):
                em.op(T, lambda e, dc=dc: e.transpose(
                    pbf(dc * 128, dc * 128 + 128), sb["xn"][:, dc * 128:dc * 128 + 128],
                    sb["idf32"][:]))
            em.sig(T, lambda e: e.matmul(pbf(3584, 3592, 0, 64), sb["onesb"][0:64, 0:64],
                                         sb["onesb"][0:64, 0:8], start=True, stop=True),
                   evt=f"tr_{tc}_7")
            if tc > 0:
                em.wait(V, f"mmg_{tc-1}")
            em.wait(V, f"tr_{tc}_7")
            for dc in range(DC):
                em.op(V, lambda e, dc=dc: e.tensor_copy(
                    sb["xnT32"][:, dc * 128:dc * 128 + 128], pbf(dc * 128, dc * 128 + 128)))
                em.sig(V, lambda e, dc=dc, tc=tc: e.tensor_copy(
                    sb["xnTb"][:, dc * 2048 + tc * 128: dc * 2048 + tc * 128 + 128],
                    pbf(dc * 128, dc * 128 + 128)),
                    evt=(f"p1ev_{tc}" if dc == DC - 1 else None))
            em.wait(T, f"p1ev_{tc}")
            if tc == 0:
                pass
            else:
                em.wait(T, f"gmask_{tc-1}")
            for dc in range(DC):
                em.sig(T, lambda e, dc=dc: e.matmul(
                    pbf(1024, 1032), sb["xnT32"][:, dc * 128:dc * 128 + 128],
                    sb["gwT32"][:, dc * 8:dc * 8 + 8], start=(dc == 0), stop=False))
            em.sig(T, lambda e: e.matmul(pbf(1024, 1032), sb["ones32"][:], sb["gbr"][:],
                                         start=False, stop=True), evt=f"mmg_{tc}")
            if tc == 0:
                em.wait(T, "ld_acat")
            else:
                em.wait(T, f"um_{tc-1}")
            for dc in range(DC):
                em.sig(T, lambda e, dc=dc, tc=tc: e.matmul(
                    pbf(1536, 1568),
                    sb["xnTb"][:, dc * 2048 + tc * 128: dc * 2048 + tc * 128 + 128],
                    sb["acatTb"][:, dc * 32:dc * 32 + 32],
                    start=(dc == 0), stop=(dc == DC - 1)),
                    evt=(f"mmu_{tc}" if dc == DC - 1 else None))
            em.wait(V, f"mmg_{tc}")
            if DEBUG and tc == TC - 1:
                em.sig(V, lambda e: e.tensor_copy(sb["dbgl"][:], pbf(1024, 1032)),
                       evt="dbg_glogcp")
            em.sig(V, lambda e: e.reduce_max(sb["stat"][:, 0:1], pbf(1024, 1032),
                                             axis=AX.X), evt=f"gmx_{tc}")
            em.wait(A, f"gmx_{tc}")
            em.sig(A, lambda e: e.copy(sb["stat"][:, 1:2], sb["stat"][:, 0:1]),
                   evt=f"gcp_{tc}")
            em.wait(V, f"gcp_{tc}")
            em.sig(V, lambda e: e.tensor_single_scalar(
                sb["gmask"][:], pbf(1024, 1032), sb["stat"][:, 1:2], ALU.is_ge),
                evt=f"gmask_{tc}")
            em.wait(A, f"gmask_{tc}")
            em.sig(A, lambda e: e.copy(sb["dbgl"][:], sb["gmask"][:]),
                   evt=f"gmcp_{tc}")
            em.wait(V, f"mmu_{tc}")
            if tc > 0:
                em.wait(V, f"mmut_{tc-1}")
            em.wait(V, f"gmcp_{tc}")
            for r in range(4):
                em.sig(V, lambda e, r=r: e.tensor_tensor(
                    bass.AP(umb_h, r, [[32, 128], [4, 8]]),
                    bass.AP(PB, 1536 + r, [[4096, 128], [4, 8]]),
                    sb["dbgl"][:],
                    ALU.mult), evt=(f"um_{tc}" if r == 3 else None))
            em.wait(T, f"um_{tc}")
            if tc > 0:
                em.wait(T, f"uT_{tc-1}")
            em.op(T, lambda e: e.transpose(pbb(4096, 4224, 0, 32), sb["umb"][:],
                                           sb["idbf"][:]))
            em.sig(T, lambda e: e.matmul(pbf(3584, 3592, 0, 64), sb["onesb"][0:64, 0:64],
                                         sb["onesb"][0:64, 0:8], start=True, stop=True),
                   evt=f"mmut_{tc}")
            em.wait(V, f"mmut_{tc}")
            em.sig(V, lambda e, tc=tc: e.tensor_copy(
                sb["uTb"][:, tc * 128:tc * 128 + 128], pbb(4096, 4224, 0, 32)),
                evt=f"uT_{tc}")
            if tc == 0:
                em.wait(T, "ld_wv"); em.wait(T, "ld_bv"); em.wait(T, "v_ones")
            else:
                em.wait(T, f"vev_{tc-1}")
            for nh in range(2):
                for dc in range(DC):
                    em.sig(T, lambda e, nh=nh, dc=dc, tc=tc: e.matmul(
                        pbf(2560 + nh * 512, 3072 + nh * 512),
                        sb["xnTb"][:, dc * 2048 + tc * 128: dc * 2048 + tc * 128 + 128],
                        sb["wvTb"][:, dc * 1024 + nh * 512: dc * 1024 + nh * 512 + 512],
                        start=(dc == 0), stop=False))
                em.sig(T, lambda e, nh=nh: e.matmul(
                    pbf(2560 + nh * 512, 3072 + nh * 512), sb["onesb"][0:1, 0:128],
                    sb["bvr"][:, nh * 512:nh * 512 + 512], start=False, stop=True),
                    evt=f"mmv_{tc}_{nh}")
            em.wait(V, f"mmv_{tc}_1")
            em.sig(V, lambda e, tc=tc: e.tensor_copy(
                bass.AP(vtile_h, tc * 1040, [[16640, 128], [65, 16], [1, 64]]),
                pbf(2560, 3584).rearrange("p (a b) -> p a b", a=16)),
                evt=f"vev_{tc}")

        if PH <= 1:
            em.sig(Sq, lambda e: e.dma_start(
                out_h.ap()[0:128, :],
                sb["xnTb"].bitcast(F32)[:, 0:2048]), evt="dummy_sto")
            em.wait(Sq, "dummy_sto", force=True)
            return
        if DEBUG:
            em.wait(G, "p1ev_15")
            em.wait(G, "rstd_15")
            em.wait(G, "dbg_glogcp")
            em.sig(G, lambda e: e.dma_start(dbg["glog"].ap(), sb["dbgl"][:]),
                   evt="dbg_glog")
            em.sig(G, lambda e: e.dma_start(dbg["stat"].ap(), sb["stat"][:]),
                   evt="dbg_stat")
            em.sig(G, lambda e: e.dma_start(dbg["xnT32"].ap(), sb["xnT32"][:]),
                   evt="dbg_xnT32")
            em.wait(G, "gmask_15")
            em.wait(G, "uT_15")
            em.sig(G, lambda e: e.dma_start(dbg["xnTb2"].ap(), sb["xnTb"][:]),
                   evt="dbg_early1")
            em.sig(G, lambda e: e.dma_start(dbg["gmask2"].ap(), sb["gmask"][:]),
                   evt="dbg_early2")
            em.sig(G, lambda e: e.dma_start(dbg["uT2"].ap(), sb["uTb"][:]),
                   evt="dbg_early3")
        last_mmB = None
        for h in range(NHEAD):
            par = h % 2
            if h == 0:
                em.wait(G, "xn_15")
                em.wait(G, "mmg_15")
            if h >= 2:
                em.wait(G, f"projdone_{h-2}")
            em.sig(G, lambda e, h=h, par=par: e.dma_start(
                sb["wqb"].rearrange("p (x a b) -> p x a b", x=2, a=DC)[:, par],
                bass.AP(dr["wqT"], h * 128, [[2048, 128], [128 * 2048, DC], [1, 128]])),
                evt=f"ldwq_{h}")
            em.sig(G, lambda e, h=h, par=par: e.dma_start(
                sb["wkb"].rearrange("p (x a b) -> p x a b", x=2, a=DC)[:, par],
                bass.AP(dr["wkT"], h * 128, [[2048, 128], [128 * 2048, DC], [1, 128]])),
                evt=f"ldwk_{h}")
            for mat, wb, rot, bc in (("q", "wqb", "qrotb", "bqc"), ("k", "wkb", "krotb", "bkc")):
                em.wait(T, f"ldw{mat}_{h}")
                if h == 0 and mat == "q":
                    em.wait(T, "p1ev_15")
                if h > 0 and mat == "q" and P2SUB == "full":
                    em.wait(T, f"m2s_{h-1}_3")
                    em.wait(T, f"att_{h-1}_3")
                for tt in range(TT):
                    for dc in range(DC):
                        em.sig(T, lambda e, dc=dc, tt=tt, wb=wb, par=par: e.matmul(
                            pbf(0, 512),
                            sb[wb][:, par * 1024 + dc * 128: par * 1024 + dc * 128 + 128],
                            sb["xnTb"][:, dc * 2048 + tt * 512: dc * 2048 + tt * 512 + 512],
                            start=(dc == 0), stop=(dc == DC - 1)),
                            evt=(f"mmA_{h}{mat}{tt}" if dc == DC - 1 else None))
                    em.wait(V, f"mmA_{h}{mat}{tt}")
                    if P2SUB == "projm":
                        em.sig(V, lambda e, rot=rot, tt=tt: e.tensor_copy(
                            sb[rot][:, tt * 512:tt * 512 + 512], pbf(0, 512)),
                            evt=f"evA_{h}{mat}{tt}")
                        em.sig(V, lambda e: None, evt=f"rope_{h}{mat}{tt}")
                        last_mmB = None
                        continue
                    if last_mmB is not None:
                        em.wait(V, last_mmB)
                    em.sig(V, lambda e, bc=bc, h=h: e.tensor_scalar_add(
                        sb["qsb"][:], pbf(0, 512), sb[bc][:, h:h + 1]),
                        evt=f"evA_{h}{mat}{tt}")
                    if P2SUB == "proja":
                        em.sig(V, lambda e, rot=rot, tt=tt: e.tensor_copy(
                            sb[rot][:, tt * 512:tt * 512 + 512], sb["qsb"][:]),
                            evt=f"rope_{h}{mat}{tt}")
                        last_mmB = None
                        continue
                    em.wait(T, f"evA_{h}{mat}{tt}")
                    em.sig(T, lambda e: e.matmul(pbf(512, 1024), sb["psig"][:], sb["qsb"][:],
                                                 start=True, stop=True),
                           evt=f"mmB_{h}{mat}{tt}")
                    last_mmB = f"mmB_{h}{mat}{tt}"
                    if P2SUB == "projb":
                        em.wait(V, f"mmB_{h}{mat}{tt}")
                        em.sig(V, lambda e, rot=rot, tt=tt: e.tensor_copy(
                            sb[rot][:, tt * 512:tt * 512 + 512], pbf(512, 1024)),
                            evt=f"rope_{h}{mat}{tt}")
                        continue
                    if h > 0 and mat == "q" and tt == 0 and P2SUB != "proj":
                        em.wait(V, f"mmS_{h-1}_3_7")
                    em.op(V, lambda e, tt=tt: e.tensor_mul(
                        sb["t1b"][:], sb["qsb"][:], sb["coss"][:, tt * 512:tt * 512 + 512]))
                    em.wait(V, f"mmB_{h}{mat}{tt}")
                    em.op(V, lambda e, tt=tt: e.tensor_mul(
                        sb["t2b"][:], pbf(512, 1024), sb["sins"][:, tt * 512:tt * 512 + 512]))
                    em.sig(V, lambda e, rot=rot, tt=tt: e.tensor_add(
                        sb[rot][:, tt * 512:tt * 512 + 512], sb["t1b"][:], sb["t2b"][:]),
                        evt=f"rope_{h}{mat}{tt}")
            em.alias(f"projdone_{h}", f"mmB_{h}k3")
            if P2SUB == "proj":
                continue
            em.wait(T, f"rope_{h}q3")
            em.wait(T, f"rope_{h}k3")

            def emit_attnv(qt, kcp, kcmax, final):
                kc0 = 2 * kcp
                slot = kcp % 2
                em.wait(T, f"mask_{h}_{qt}_{kcp}")
                if kcp == 0:
                    if qt > 0:
                        em.wait(T, f"vcomb_{h}_{qt-1}")
                    elif h > 0:
                        em.wait(T, f"vcomb_{h-1}_3")
                    else:
                        em.wait(T, "vev_15")
                for mi in range(2):
                    eb = "e1b" if mi == 0 else "e2b"
                    pc = 3072 + mi * 512
                    for j, kc in ((0, kc0), (1, kc0 + 1)):
                        em.sig(T, lambda e, kc=kc, j=j, pc=pc, eb=eb, slot=slot: e.matmul(
                            pbf(pc, pc + 512, 0, 65),
                            sb["vtile"][:, kc * 1040 + h * 65: kc * 1040 + h * 65 + 65],
                            sb[eb][:, slot * 1024 + j * 512: slot * 1024 + j * 512 + 512],
                            start=(kc == 0), stop=(kc == kcmax - 1)),
                            evt=(f"mmPp_{h}_{qt}_{kcp}" if (mi == 1 and j == 1) else None))

            for qt in range(TT):
                kcmax = 4 * qt + 4
                npair = kcmax // 2
                for kcp in range(npair):
                    kc0 = 2 * kcp
                    slot = kcp % 2
                    if kcp > 0:
                        em.wait(T, f"exp_{h}_{qt}_{kcp-1}")
                    elif qt > 0:
                        em.wait(T, f"exp_{h}_{qt-1}_{(4*(qt-1)+4)//2-1}")
                    elif h > 0:
                        em.wait(T, f"exp_{h-1}_3_7")
                    else:
                        em.wait(T, "gmask_15"); em.wait(T, "um_15")
                        em.wait(T, "uT_15"); em.wait(T, "vev_15")
                    for mi in range(2):
                        base = mi * 64
                        scol = 1024 + mi * 1024
                        for j, kc in ((0, kc0), (1, kc0 + 1)):
                            em.sig(T, lambda e, base=base, scol=scol, j=j, kc=kc, qt=qt: e.matmul(
                                pbf(scol + j * 512, scol + j * 512 + 512),
                                sb["krotb"][base:base + 64, kc * 128:kc * 128 + 128],
                                sb["qrotb"][base:base + 64, qt * 512:qt * 512 + 512],
                                start=True, stop=True),
                                evt=(f"mmS_{h}_{qt}_{kcp}" if (mi == 1 and j == 1) else None))
                    em.wait(A, f"mmS_{h}_{qt}_{kcp}")
                    if P2SUB == "scores":
                        em.wait(A, f"mask_{h}_{qt}_{kcp-1}") if kcp > 0 else None
                    elif kcp >= 2:
                        em.wait(A, f"mmPp_{h}_{qt}_{kcp-2}")
                    elif qt > 0:
                        em.wait(A, f"mmPp_{h}_{qt-1}_{(4*(qt-1)+4)//2-1}")
                    elif h > 0:
                        em.wait(A, f"mmPp_{h-1}_3_7")
                    em.sig(A, lambda e, slot=slot: e.activation(
                        sb["e1b"][:, slot * 1024: slot * 1024 + 1024], pbf(1024, 2048),
                        ACTF.Exp, scale=0.125))
                    em.sig(A, lambda e, slot=slot: e.activation(
                        sb["e2b"][:, slot * 1024: slot * 1024 + 1024], pbf(2048, 3072),
                        ACTF.Exp, scale=0.125), evt=f"exp_{h}_{qt}_{kcp}")
                    em.wait(V, f"exp_{h}_{qt}_{kcp}")
                    mops = []
                    for j, kc in ((0, kc0), (1, kc0 + 1)):
                        doff = kc - 4 * qt
                        if doff >= 0:
                            mc = doff * 128
                            for eb in ("e1b", "e2b"):
                                col0 = slot * 1024 + j * 512
                                if mc > 0:
                                    mops.append(("ms", eb, col0, mc))
                                mops.append(("tm", eb, col0, mc))
                    for i, (kind, eb, col0, mc) in enumerate(mops):
                        last = (i == len(mops) - 1)
                        ev = f"mask_{h}_{qt}_{kcp}" if last else None
                        if kind == "ms":
                            em.sig(V, lambda e, eb=eb, col0=col0, mc=mc: e.tensor_copy(
                                sb[eb][:, col0: col0 + mc], sb["zerob"][:, 0:mc]), evt=ev)
                        else:
                            em.sig(V, lambda e, eb=eb, col0=col0, mc=mc: e.tensor_mul(
                                sb[eb][:, col0 + mc: col0 + mc + 128],
                                sb[eb][:, col0 + mc: col0 + mc + 128],
                                sb["trim"][:]), evt=ev)
                    if not mops:
                        em.alias(f"mask_{h}_{qt}_{kcp}", f"exp_{h}_{qt}_{kcp}")
                    if P2SUB != "scores" and kcp > 0:
                        emit_attnv(qt, kcp - 1, kcmax, False)
                    elif P2SUB != "scores" and kcp == 0 and qt > 0:
                        emit_epilogue(qt - 1)
                if P2SUB == "scores":
                    continue
                emit_attnv(qt, npair - 1, kcmax, True)

                def emit_epilogue(qt):
                    lastp = 2 * qt + 1
                    em.wait(V, f"mmPp_{h}_{qt}_{lastp}")
                    prev = (h, qt - 1) if qt > 0 else (h - 1, 3)
                    if prev[0] >= 0:
                        em.wait(V, f"bcRS_{prev[0]}_{prev[1]}")
                        em.wait(V, f"rs_{prev[0]}_{prev[1]}")
                    em.op(V, lambda e: e.tensor_copy(sb["rbf"][64:65, 0:512],
                                                     pbf(3584, 4096, 64, 65)))
                    em.op(V, lambda e: e.scalar_tensor_tensor(
                        sb["rtmp"][64:65, 0:512], pbf(3072, 3584, 64, 65), 1e-3,
                        sb["rbf"][64:65, 0:512], ALU.mult, ALU.mult))
                    em.wait(V, "lam_bc")
                    em.op(V, lambda e, h=h: e.tensor_scalar_mul(
                        sb["rbf"][64:65, 512:1024], pbf(3072, 3584, 64, 65),
                        sb["lbc"][64:65, h:h + 1]))
                    em.sig(V, lambda e: e.tensor_mul(
                        sb["rtmp"][64:65, 512:1024], sb["rtmp"][64:65, 0:512],
                        sb["rtmp"][64:65, 0:512]), evt=f"rows_{h}_{qt}")
                    em.wait(T, f"rows_{h}_{qt}")
                    if prev[0] >= 0:
                        em.wait(T, f"m2s_{prev[0]}_{prev[1]}")
                        em.wait(T, f"att_{prev[0]}_{prev[1]}")
                    em.sig(T, lambda e: e.matmul(pbf(0, 512, 0, 64), sb["onesb"][64:65, 0:64],
                                                 sb["rbf"][64:65, 0:512], start=True, stop=True),
                           evt=f"bcA_{h}_{qt}")
                    em.sig(T, lambda e: e.matmul(pbf(512, 1024, 0, 64), sb["onesb"][64:65, 0:64],
                                                 sb["rbf"][64:65, 512:1024], start=True, stop=True),
                           evt=f"bcB_{h}_{qt}")
                    em.wait(V, f"bcB_{h}_{qt}")
                    em.op(V, lambda e: e.tensor_copy(sb["ct"][0:64, :], pbf(0, 512, 0, 64)))
                    em.op(V, lambda e: e.tensor_mul(sb["cv"][0:64, :], pbf(3072, 3584, 0, 64),
                                                    sb["ct"][0:64, :]))
                    em.op(V, lambda e: e.tensor_copy(sb["ct"][0:64, :], pbf(512, 1024, 0, 64)))
                    em.op(V, lambda e: e.tensor_mul(sb["m2s"][0:64, :], pbf(3584, 4096, 0, 64),
                                                    sb["ct"][0:64, :]))
                    em.sig(V, lambda e: e.tensor_sub(sb["cv"][0:64, :], sb["cv"][0:64, :],
                                                     sb["m2s"][0:64, :]), evt=f"vcomb_{h}_{qt}")
                    em.sig(V, lambda e: e.tensor_mul(sb["v2b"][0:64, :], sb["cv"][0:64, :],
                                                     sb["cv"][0:64, :]), evt=f"v2_{h}_{qt}")
                    em.wait(T, f"v2_{h}_{qt}")
                    em.sig(T, lambda e: e.matmul(pbf(0, 512, 0, 64), sb["onesb"][0:64, 0:64],
                                                 sb["v2b"][0:64, :], start=True, stop=True),
                           evt=f"mmM2_{h}_{qt}")
                    em.wait(V, f"mmM2_{h}_{qt}")
                    em.sig(V, lambda e: e.tensor_copy(sb["m2s"][64:65, :], pbf(0, 512, 0, 1)),
                           evt=f"m2s_{h}_{qt}")
                    em.sig(V, lambda e: e.scalar_tensor_tensor(
                        sb["zt"][64:65, 0:512], sb["m2s"][64:65, :], 1.0 / 64,
                        sb["rtmp"][64:65, 512:1024], ALU.mult, ALU.add), evt=f"z_{h}_{qt}")
                    em.wait(A, f"z_{h}_{qt}")
                    em.op(A, lambda e: e.activation(sb["zt"][64:65, 512:1024],
                                                    sb["zt"][64:65, 0:512], ACTF.Ln))
                    em.sig(A, lambda e: e.activation(sb["rsb"][64:65, :],
                                                     sb["zt"][64:65, 512:1024], ACTF.Exp,
                                                     scale=-0.5), evt=f"rs_{h}_{qt}")
                    em.wait(T, f"rs_{h}_{qt}")
                    em.sig(T, lambda e: e.matmul(pbf(512, 1024, 0, 64), sb["onesb"][64:65, 0:64],
                                                 sb["rsb"][64:65, 0:512], start=True, stop=True),
                           evt=f"bcRS_{h}_{qt}")
                    em.wait(V, f"bcRS_{h}_{qt}")
                    hc, hodd = h // 2, h % 2
                    acol = hc * 2048 + qt * 512
                    if hodd == 0:
                        em.sig(V, lambda e, acol=acol: e.tensor_mul(
                            sb["attoT"][0:64, acol:acol + 512], sb["cv"][0:64, :],
                            pbf(512, 1024, 0, 64)), evt=f"att_{h}_{qt}")
                    else:
                        em.op(V, lambda e: e.tensor_mul(
                            sb["atmp"][0:64, :], sb["cv"][0:64, :], pbf(512, 1024, 0, 64)))
                        em.sig(V, lambda e, acol=acol: e.tensor_copy(
                            sb["attoT"][64:128, acol:acol + 512], sb["atmp"][0:64, :]),
                            evt=f"att_{h}_{qt}")
                if qt == TT - 1:
                    emit_epilogue(qt)

        if PH <= 2:
            em.sig(Sq, lambda e: e.dma_start(
                out_h.ap()[0:128, :],
                sb["attoT"].bitcast(F32)[:, 0:2048]), evt="dummy_sto")
            em.wait(Sq, "dummy_sto", force=True)
            return
        for dc in range(DC):
            par = dc % 2
            if dc >= 2:
                em.wait(G, f"mmST_{dc-2}_3")
            em.sig(G, lambda e, dc=dc, par=par: e.dma_start(
                sb["wost"].rearrange("p (x a b) -> p x a b", x=2, a=DC)[:, par],
                bass.AP(dr["woT"], dc * 128, [[D, 128], [128 * D, DC], [1, 128]])),
                evt=f"ldwo_{dc}")
            em.wait(V, f"ldwo_{dc}")
            if dc >= 2:
                em.wait(V, f"mmST_{dc-2}_3")
            for dcin in range(DC):
                em.sig(V, lambda e, dcin=dcin, par=par: e.tensor_scalar_mul(
                    sb["wost"][:, par * 1024 + dcin * 128: par * 1024 + dcin * 128 + 128],
                    sb["wost"][:, par * 1024 + dcin * 128: par * 1024 + dcin * 128 + 128],
                    sb["hnwc"][:, dcin:dcin + 1]),
                    evt=(f"wosc_{dc}" if dcin == DC - 1 else None))
            for tt in range(TT):
                idx = dc * TT + tt
                ppar = idx % 2
                em.wait(T, f"att_{H-1}_3")
                em.wait(T, f"exp_{H-1}_3_7")
                em.wait(T, f"wosc_{dc}")
                if idx >= 2:
                    pdc, ptt = (idx - 2) // TT, (idx - 2) % TT
                    em.wait(T, f"sev_{pdc}_{ptt}")
                for dcin in range(DC):
                    em.sig(T, lambda e, dcin=dcin, tt=tt, par=par, ppar=ppar: e.matmul(
                        pbf(1024 + ppar * 512, 1536 + ppar * 512),
                        sb["wost"][:, par * 1024 + dcin * 128: par * 1024 + dcin * 128 + 128],
                        sb["attoT"][:, dcin * 2048 + tt * 512: dcin * 2048 + tt * 512 + 512],
                        start=(dcin == 0), stop=(dcin == DC - 1)),
                        evt=(f"mmST_{dc}_{tt}" if dcin == DC - 1 else None))
                em.wait(V, f"mmST_{dc}_{tt}")
                em.sig(V, lambda e, dc=dc, tt=tt, ppar=ppar: e.tensor_scalar_add(
                    sb["sTb"][:, dc * 2048 + tt * 512: dc * 2048 + tt * 512 + 512],
                    pbf(1024 + ppar * 512, 1536 + ppar * 512), sb["boc"][:, dc:dc + 1]),
                    evt=f"sev_{dc}_{tt}")

        if PH <= 3:
            em.sig(Sq, lambda e: e.dma_start(
                out_h.ap()[0:128, :],
                sb["sTb"].bitcast(F32)[:, 0:2048]), evt="dummy_sto")
            em.wait(Sq, "dummy_sto", force=True)
            return
        for tt in range(TT):
            for fc in range(FC):
                idx = tt * FC + fc
                par = idx % 2
                if idx == 0:
                    em.wait(G, f"mmST_{DC-1}_3")
                    em.wait(G, f"att_{H-1}_3")
                    em.wait(G, f"rs_{H-1}_3")
                if idx >= 2:
                    em.wait(G, f"mmI_{idx-2}")
                em.sig(G, lambda e, fc=fc, par=par: e.dma_start(
                    sb["wib"].rearrange("p (x a b) -> p x a b", x=2, a=DC)[:, par],
                    bass.AP(dr["wiT"], fc * 128, [[F, 128], [128 * F, DC], [1, 128]])),
                    evt=f"ldwi_{idx}")
                em.wait(T, f"ldwi_{idx}")
                if idx == 0:
                    em.wait(T, "ld_bct")
                    em.wait(T, "uT_15")
                    em.wait(T, f"m2s_{H-1}_3")
                    em.wait(T, f"att_{H-1}_3")
                if idx >= 2:
                    em.wait(T, f"iev_{idx-2}")
                for dc in range(DC):
                    em.sig(T, lambda e, dc=dc, tt=tt, par=par: e.matmul(
                        pbf(par * 512, par * 512 + 512),
                        sb["wib"][:, par * 1024 + dc * 128: par * 1024 + dc * 128 + 128],
                        sb["xnTb"][:, dc * 2048 + tt * 512: dc * 2048 + tt * 512 + 512],
                        start=(dc == 0), stop=False))
                em.sig(T, lambda e, fc=fc, tt=tt, par=par: e.matmul(
                    pbf(par * 512, par * 512 + 512),
                    sb["bctb"][0:32, fc * 128:fc * 128 + 128],
                    sb["uTb"][0:32, tt * 512:tt * 512 + 512],
                    start=False, stop=True), evt=f"mmI_{idx}")
                em.wait(V, f"mmI_{idx}")
                if idx == 0:
                    em.wait(V, f"mmPp_{H-1}_3_7")
                if tt > 0:
                    em.wait(V, f"mmG_{(tt-1)*DC + DC - 1}")
                em.sig(V, lambda e, fc=fc, par=par: e.tensor_scalar(
                    sb["interTb"][:, fc * 512:fc * 512 + 512],
                    pbf(par * 512, par * 512 + 512),
                    sb["wibc"][:, fc:fc + 1], 0.0, ALU.add, ALU.max), evt=f"iev_{idx}")
            for dc in range(DC):
                gidx = tt * DC + dc
                par = gidx % 2
                if gidx >= 2:
                    em.wait(G, f"mmG_{gidx-2}")
                em.sig(G, lambda e, dc=dc, par=par: e.dma_start(
                    sb["fwoTb"].rearrange("p (x a b) -> p x a b", x=2, a=FC)[:, par],
                    bass.AP(dr["fwoT"], dc * 128, [[D, 128], [128 * D, FC], [1, 128]])),
                    evt=f"ldfw_{gidx}")
                if gidx == 0:
                    em.wait(Sq, f"z_{H-1}_3")
                if gidx >= 2:
                    em.wait(Sq, f"oev_{gidx-2}")
                em.sig(Sq, lambda e, dc=dc, tt=tt, par=par: e.dma_start(
                    sb["xTs"][:, par * 512:(par + 1) * 512],
                    dr["xT"].ap()[dc * 128:(dc + 1) * 128, tt * 512:(tt + 1) * 512]),
                    evt=f"ldxT_{gidx}")
                em.wait(T, f"ldfw_{gidx}")
                em.wait(T, f"iev_{tt * FC + FC - 1}")
                if gidx >= 2:
                    em.wait(T, f"gev_{gidx-2}")
                for fc in range(FC):
                    em.sig(T, lambda e, fc=fc, par=par: e.matmul(
                        pbf(2048 + par * 512, 2560 + par * 512),
                        sb["fwoTb"][:, par * 4096 + fc * 128: par * 4096 + fc * 128 + 128],
                        sb["interTb"][:, fc * 512:fc * 512 + 512],
                        start=(fc == 0), stop=(fc == FC - 1)),
                        evt=(f"mmG_{gidx}" if fc == FC - 1 else None))
                em.wait(A, f"mmG_{gidx}")
                if gidx >= 2:
                    em.wait(A, f"oev_{gidx-2}")
                em.sig(A, lambda e, dc=dc, par=par: e.activation(
                    sb["gbuf"][:, par * 512:(par + 1) * 512],
                    pbf(2048 + par * 512, 2560 + par * 512), ACTF.Silu,
                    bias=sb["wobc"][:, dc:dc + 1]), evt=f"gev_{gidx}")
                em.wait(V, f"gev_{gidx}")
                em.wait(V, f"ldxT_{gidx}")
                if gidx == 0:
                    em.wait(V, f"bcB_{H-1}_3")
                if gidx >= 2:
                    em.wait(V, f"sto_{gidx-2}")
                em.op(V, lambda e, dc=dc, tt=tt, par=par: e.tensor_mul(
                    sb["tbuf"][:], sb["sTb"][:, dc * 2048 + tt * 512: dc * 2048 + tt * 512 + 512],
                    sb["gbuf"][:, par * 512:(par + 1) * 512]))
                em.sig(V, lambda e, par=par: e.tensor_add(
                    sb["outb"][:, par * 512:(par + 1) * 512], sb["tbuf"][:],
                    sb["xTs"][:, par * 512:(par + 1) * 512]), evt=f"oev_{gidx}")
                em.wait(Sq, f"oev_{gidx}")
                em.sig(Sq, lambda e, dc=dc, tt=tt, par=par: e.dma_start(
                    out_h.ap()[dc * 128:(dc + 1) * 128, tt * 512:(tt + 1) * 512],
                    sb["outb"][:, par * 512:(par + 1) * 512]), evt=f"sto_{gidx}")

        if DEBUG:
            em.wait(G, f"sto_{TT * DC - 1}")
            dbg_list = [("xnTb", "xnTb"), ("v", "vtile"), ("uT", "uTb"),
                        ("qrot", "qrotb"), ("krot", "krotb"), ("att", "attoT"),
                        ("sT", "sTb"), ("inter", "interTb"), ("gmask", "gmask")]
            for nm, src in dbg_list:
                em.sig(G, lambda e, nm=nm, src=src: e.dma_start(
                    dbg[nm].ap(), sb[src][:]), evt=f"dbg_{nm}")
            em.wait(G, f"dbg_{dbg_list[-1][0]}", force=True)
        for gidx in range(TT * DC - NDMASEM, TT * DC):
            em.wait(Sq, f"sto_{gidx}", force=True)

    with es:
        block = es.enter_context(nc.Block())
        prog(Em("count", sems, events))

        @block.tensor
        def _(eng):
            prog(Em("T", sems, events, eng))

        @block.vector
        def _(eng):
            prog(Em("V", sems, events, eng))

        @block.scalar
        def _(eng):
            prog(Em("A", sems, events, eng))

        @block.gpsimd
        def _(eng):
            prog(Em("G", sems, events, eng))

        @block.sync
        def _(eng):
            prog(Em("S", sems, events, eng))

    return nc


_NC_CACHE = None


def kernel(**inputs):
    global _NC_CACHE
    consts = _host_consts()
    w = _prep_weights(inputs)
    x = np.asarray(inputs["L0_attn_x"], np.float32)
    if _NC_CACHE is None:
        _NC_CACHE = build_nc()
    nc = _NC_CACHE
    in_maps = []
    for c in range(NCORE):
        m = {}
        m.update(consts)
        m.update(w)
        xc = np.ascontiguousarray(x[c])
        m["x"] = xc
        m["xT"] = np.ascontiguousarray(xc.T)
        in_maps.append(m)
    res = run_bass_kernel_spmd(nc, in_maps, core_ids=list(range(NCORE)))
    out = np.stack([np.ascontiguousarray(r["outT"].T) for r in res.results], axis=0)
    return out


# version 03 (final)
# speedup vs baseline: 1.0298x; 1.0227x over previous
import numpy as np
import ml_dtypes
import concourse.bass as bass
import concourse.mybir as mybir
from concourse.bass_utils import run_bass_kernel_spmd
from contextlib import ExitStack

F32 = mybir.dt.float32
BF16 = mybir.dt.bfloat16
ALU = mybir.AluOpType
ACTF = mybir.ActivationFunctionType
AX = mybir.AxisListType
BF = ml_dtypes.bfloat16

S, D, H = 2048, 1024, 16
TC, TT, DC, FC = 16, 4, 8, 32
F = 4096
NCORE = 8
DEBUG = False
PH = 4
P2SUB = 'full'
NHEAD = H


def _host_consts():
    j = np.arange(32)
    inv = 10000.0 ** (-(j.astype(np.float64)) / 32.0)
    t = np.arange(S, dtype=np.float64)
    ang = t[None, :] * inv[:, None]
    p = np.arange(128)
    coss = np.cos(ang)[p % 32, :]
    sins = np.sin(ang)[p % 32, :] * np.where((p % 64) < 32, -1.0, 1.0)[:, None]
    psig = np.zeros((128, 128))
    psig[p, p ^ 32] = 1.0
    trim = (np.arange(128)[None, :] >= np.arange(128)[:, None]).astype(np.float64)
    return {
        "coss": coss.astype(np.float32).astype(BF),
        "sins": sins.astype(np.float32).astype(BF),
        "psig": psig.astype(BF),
        "trim": trim.astype(BF),
        "onesb": np.ones((128, 128), BF),
        "ones32": np.ones((1, 128), np.float32),
        "idf32": np.eye(128, dtype=np.float32),
        "idbf": np.eye(128, dtype=np.float32).astype(BF),
        "eps5": np.full((128, 1), 1e-5, np.float32),
        "zerob": np.zeros((128, 384), BF),
    }


def _prep_weights(I):
    g = lambda k: np.asarray(I[k], np.float32)
    w = {}
    w["wqT"] = g("L0_attn_Wq").T
    w["wkT"] = g("L0_attn_Wk").T
    w["wvT"] = g("L0_attn_Wv").T
    w["woT"] = g("L0_attn_Wo").T
    w["wiT"] = g("L0_ffn_wi_W").T
    w["fwoT"] = g("L0_ffn_wo_W").T
    w["gwT"] = g("L0_ffn_gate_W").T
    w["acatT"] = g("L0_ffn_lora_A").reshape(32, 1024).T
    w["bcatT"] = g("L0_ffn_lora_B").transpose(1, 0, 2).reshape(4096, 32).T
    w["bq_c"] = g("L0_attn_bq").reshape(16, 128).T
    w["bk_c"] = g("L0_attn_bk").reshape(16, 128).T
    w["bv_r"] = g("L0_attn_bv").reshape(1, 1024)
    w["bo_c"] = g("L0_attn_bo").reshape(8, 128).T
    w["wib_c"] = g("L0_ffn_wi_b").reshape(32, 128).T
    w["wob_c"] = g("L0_ffn_wo_b").reshape(8, 128).T
    w["gb_r"] = g("L0_ffn_gate_b").reshape(1, 8)
    w["hnw_c"] = g("L0_attn_head_norm_w").reshape(1024).reshape(8, 128).T
    w["lamA"] = np.concatenate([g("L0_attn_lq1"), g("L0_attn_lq2")], axis=1)
    w["lamB"] = np.concatenate([g("L0_attn_lk1"), g("L0_attn_lk2")], axis=1)
    return {k: np.ascontiguousarray(v) for k, v in w.items()}


NDMASEM = 8


class Em:
    CSEM = {"T": "tT", "V": "tV", "A": "tA"}

    def __init__(self, mode, sems, events, eng=None):
        self.mode = mode
        self.sems = sems
        self.events = events
        self.eng = eng
        self.counts = {}
        self.qidx = {"S": 0, "G": 0}
        self.waited = {}
        self.own_sems = set()
        if mode in self.CSEM:
            self.own_sems.add(self.CSEM[mode])
        elif mode in ("S", "G"):
            self.own_sems.update(f"t{mode}{i}" for i in range(NDMASEM))

    def op(self, engine, fn):
        if self.mode == engine:
            fn(self.eng)

    def sig(self, engine, fn, evt=None):
        if engine in ("S", "G"):
            slot = self.qidx[engine] % NDMASEM
            self.qidx[engine] += 1
            sem = f"t{engine}{slot}"
            prev = self.counts.get(sem, 0)
            if self.mode == "count":
                self.counts[sem] = prev + 16
                if evt is not None:
                    assert evt not in self.events, f"dup evt {evt}"
                    self.events[evt] = (sem, prev + 16)
            elif self.mode == engine:
                if prev > 0:
                    self.eng.wait_ge(self.sems[sem], prev)
                    self.waited[sem] = max(self.waited.get(sem, -1), prev)
                self.counts[sem] = prev + 16
                fn(self.eng).then_inc(self.sems[sem], 16)
            else:
                self.counts[sem] = prev + 16
            return
        sem = self.CSEM[engine]
        if self.mode == "count":
            self.counts[sem] = self.counts.get(sem, 0) + 1
            if evt is not None:
                assert evt not in self.events, f"dup evt {evt}"
                self.events[evt] = (sem, self.counts[sem])
        elif self.mode == engine:
            ins = fn(self.eng)
            assert ins is not None, "sig() requires a real instruction (nop+inc faults HW)"
            ins.then_inc(self.sems[sem], 1)

    def alias(self, evt_new, evt_old):
        if self.mode == "count":
            assert evt_new not in self.events and evt_old in self.events
            self.events[evt_new] = self.events[evt_old]

    def wait(self, engine, evt, force=False):
        if self.mode == "count":
            assert evt in self.events, f"wait on future evt {evt}"
            return
        if self.mode == engine:
            sem, val = self.events[evt]
            if (not force) and sem in self.own_sems and sem in self.CSEM.values():
                return
            if self.waited.get(sem, -1) >= val:
                return
            self.waited[sem] = val
            self.eng.wait_ge(self.sems[sem], val)


def build_nc():
    nc = bass.Bass("TRN2", target_bir_lowering=False, detect_race_conditions=False)
    dr = {}

    def din(name, shape, dt=F32):
        dr[name] = nc.dram_tensor(name, list(shape), dt, kind="ExternalInput")

    din("x", (S, D)); din("xT", (D, S))
    din("wqT", (D, 2 * D)); din("wkT", (D, 2 * D)); din("wvT", (D, D)); din("woT", (D, D))
    din("wiT", (D, F)); din("fwoT", (F, D)); din("gwT", (D, 8))
    din("acatT", (D, 32)); din("bcatT", (32, F))
    din("bq_c", (128, 16)); din("bk_c", (128, 16)); din("bv_r", (1, D))
    din("bo_c", (128, 8)); din("wib_c", (128, 32)); din("wob_c", (128, 8))
    din("gb_r", (1, 8)); din("hnw_c", (128, 8))
    din("lamA", (16, 128)); din("lamB", (16, 128)); din("eps5", (128, 1))
    din("coss", (128, S), BF16); din("sins", (128, S), BF16)
    din("psig", (128, 128), BF16); din("trim", (128, 128), BF16)
    din("onesb", (128, 128), BF16); din("zerob", (128, 384), BF16); din("ones32", (1, 128)); din("idf32", (128, 128))
    din("idbf", (128, 128), BF16)
    out_h = nc.dram_tensor("outT", [D, S], F32, kind="ExternalOutput")
    dbg = {}
    if DEBUG:
        for nm, shp in [("xnTb", (128, 16384)), ("v", (128, 16640)),
                        ("uT", (32, 2048)), ("qrot", (128, 2048)),
                        ("krot", (128, 2048)), ("att", (128, 16384)),
                        ("sT", (128, 16384)), ("inter", (128, 16384)),
                        ("gmask", (128, 8)), ("xnTb2", (128, 16384)),
                        ("gmask2", (128, 8)), ("uT2", (32, 2048)),
                        ("glog", (128, 8)), ("stat", (128, 8)),
                        ("xnT32", (128, 1024)), ("xn0", (128, 1024)),
                        ("xs0", (128, 1024)), ("st0", (128, 8))]:
            dbg[nm] = nc.dram_tensor("d_" + nm, list(shp), F32, kind="ExternalOutput")

    es = ExitStack()
    sb = {}

    def tile(name, shape, dt):
        h = es.enter_context(nc.sbuf_tensor("t_" + name, list(shape), dt))
        sb[name] = h.ap()
        return h

    tile("xnTb", (128, 16384), BF16)
    tile("attoT", (128, 16384), BF16)
    tile("sTb", (128, 16384), BF16)
    tile("uTb", (32, 2048), BF16)
    vtile_h = tile("vtile", (128, 16640), BF16)
    tile("coss", (128, S), BF16); tile("sins", (128, S), BF16)
    tile("psig", (128, 128), BF16); tile("trim", (128, 128), BF16)
    tile("onesb", (128, 128), BF16); tile("zerob", (128, 384), BF16); tile("ones32", (1, 128), F32)
    tile("idf32", (128, 128), F32); tile("idbf", (128, 128), BF16)
    tile("bqc", (128, 16), F32); tile("bkc", (128, 16), F32)
    tile("bvr", (1, D), BF16); tile("boc", (128, 8), F32)
    tile("wibc", (128, 32), F32); tile("wobc", (128, 8), F32)
    tile("gbr", (1, 8), F32); tile("hnwc", (128, 8), F32)
    tile("lamA", (16, 128), F32); tile("lamB", (16, 128), F32)
    tile("lprod", (16, 128), F32); tile("lsc", (16, 8), F32)
    tile("lbc", (128, 16), F32)
    tile("gwT32", (128, 64), F32)
    tile("acatTb", (128, 256), BF16)
    tile("bctb", (32, F), BF16)
    tile("stat", (128, 8), F32)
    tile("eps5", (128, 1), F32)
    tile("gmask", (128, 8), F32)
    tile("dbgl", (128, 8), F32)
    umb_h = tile("umb", (128, 32), BF16)

    ARN = es.enter_context(nc.sbuf_tensor("ARN", [128, 12800], F32))
    ARNf = ARN.ap()
    ARNb = ARN.ap().bitcast(BF16)

    def af(byte_off, cols):
        assert byte_off % 4 == 0
        return ARNf[:, byte_off // 4: byte_off // 4 + cols]

    def ab(byte_off, cols):
        assert byte_off % 2 == 0
        return ARNb[:, byte_off // 2: byte_off // 2 + cols]

    sb["xst"] = af(0, 2048)
    sb["xsq"] = af(8192, 1024)
    sb["xn"] = af(12288, 1024)
    sb["xnT32"] = af(16384, 1024)
    sb["wvTb"] = ab(20480, 8192)
    sb["wqb"] = ab(0, 2048)
    sb["wkb"] = ab(4096, 2048)
    sb["qsb"] = ab(8192, 512)
    sb["t1b"] = ab(9216, 512)
    sb["t2b"] = ab(10240, 512)
    sb["qrotb"] = ab(11264, 2048)
    sb["krotb"] = ab(15360, 2048)
    sb["ecat"] = ab(19456, 4096)
    sb["rtmp"] = af(27648, 1024)
    sb["rbf"] = ab(31744, 1024)
    sb["ct"] = af(33792, 512)
    sb["cv"] = af(35840, 512)
    sb["v2b"] = ab(37888, 512)
    sb["zt"] = af(38912, 1024)
    sb["m2s"] = af(43008, 512)
    sb["rsb"] = ab(45056, 512)
    sb["atmp"] = ab(46080, 512)
    sb["wost"] = ab(47104, 2048)
    sb["interTb"] = vtile_h.ap()[:, 0:16384]
    sb["wib"] = ab(0, 2048)
    sb["fwoTb"] = ab(4096, 8192)
    sb["gbuf"] = af(20480, 1024)
    sb["tbuf"] = af(24576, 512)
    sb["xTs"] = af(26624, 1024)
    sb["outb"] = af(30720, 1024)

    PB = es.enter_context(nc.psum_tensor("PB", [128, 4096], F32))

    def pbf(c0, c1, p0=0, p1=128):
        return PB[p0:p1, c0:c1]

    def pbb(c0b, c1b, p0=0, p1=128):
        return PB.ap().bitcast(BF16)[p0:p1, c0b:c1b]

    sems = {}
    semnames = ["tT", "tV", "tA"] + [f"tS{i}" for i in range(NDMASEM)] \
        + [f"tG{i}" for i in range(NDMASEM)]
    for sname in semnames:
        sems[sname] = es.enter_context(nc.semaphore(sname))
    events = {}

    def prog(em: Em):
        V, T, A, G, Sq = "V", "T", "A", "G", "S"

        small = [("coss", "coss"), ("sins", "sins"), ("psig", "psig"),
                 ("trim", "trim"), ("onesb", "onesb"), ("ones32", "ones32"),
                 ("idf32", "idf32"), ("idbf", "idbf"),
                 ("bq_c", "bqc"), ("bk_c", "bkc"), ("bo_c", "boc"),
                 ("wib_c", "wibc"), ("wob_c", "wobc"), ("gb_r", "gbr"),
                 ("hnw_c", "hnwc"), ("lamA", "lamA"), ("lamB", "lamB"),
                 ("eps5", "eps5"), ("zerob", "zerob")]
        for i, (nm, dst) in enumerate(small):
            em.sig(Sq, lambda e, nm=nm, dst=dst: e.dma_start(sb[dst][:], dr[nm].ap()),
                   evt=f"lds_{i}")
        em.sig(Sq, lambda e: e.dma_start(
            sb["gwT32"].rearrange("p (a b) -> p a b", a=DC),
            bass.AP(dr["gwT"], 0, [[8, 128], [128 * 8, DC], [1, 8]])), evt="ld_gw")

        def ld_barrier(engine):
            for i in range(len(small) + 1 - NDMASEM, len(small)):
                em.wait(engine, f"lds_{i}")
            em.wait(engine, "ld_gw")
        em.sig(G, lambda e: e.dma_start(
            sb["wvTb"].rearrange("p (a b) -> p a b", a=DC),
            bass.AP(dr["wvT"], 0, [[D, 128], [128 * D, DC], [1, D]])), evt="ld_wv")
        em.sig(G, lambda e: e.dma_start(
            sb["acatTb"].rearrange("p (a b) -> p a b", a=DC),
            bass.AP(dr["acatT"], 0, [[32, 128], [128 * 32, DC], [1, 32]])), evt="ld_acat")
        em.sig(G, lambda e: e.dma_start(sb["bvr"][:], dr["bv_r"].ap()), evt="ld_bv")
        em.sig(G, lambda e: e.dma_start(sb["bctb"][:], dr["bcatT"].ap()), evt="ld_bct")

        ld_barrier("A")

        ld_barrier("V")
        em.op(V, lambda e: e.tensor_scalar_mul(sb["hnwc"][:], sb["hnwc"][:], 0.8))
        em.sig(V, lambda e: e.memset(vtile_h[:], 1.0), evt="v_ones")
        em.op(V, lambda e: e.tensor_mul(sb["lprod"][:], sb["lamA"][:], sb["lamB"][:]))
        em.op(V, lambda e: e.reduce_sum(sb["lsc"][:, 0:1], sb["lprod"][:, 0:64], axis=AX.X))
        em.sig(V, lambda e: e.reduce_sum(sb["lsc"][:, 1:2], sb["lprod"][:, 64:128], axis=AX.X),
               evt="lam_s")
        em.wait(A, "lam_s")
        em.sig(A, lambda e: e.activation(sb["lsc"][:, 2:4], sb["lsc"][:, 0:2], ACTF.Exp),
               evt="lam_e")
        em.wait(V, "lam_e")
        em.sig(V, lambda e: e.scalar_tensor_tensor(
            sb["lsc"][:, 5:6], sb["lsc"][:, 2:3], 0.2, sb["lsc"][:, 3:4],
            ALU.add, ALU.subtract), evt="lam_c")
        ld_barrier("T")
        em.wait(Sq, "lam_c")
        em.sig(Sq, lambda e: e.dma_start(sb["lbc"][64:65, 0:16], sb["lsc"][:, 5:6]),
               evt="lam_bc")

        if PH <= 0:
            em.sig(Sq, lambda e: e.dma_start(
                out_h.ap()[0:128, :],
                sb["xnTb"].bitcast(F32)[:, 0:2048]), evt="dummy_sto")
            em.wait(Sq, "dummy_sto", force=True)
            return
        for tc in range(TC):
            par = tc % 2
            xs = sb["xst"][:, par * D:(par + 1) * D]
            if tc >= 2:
                em.wait(Sq, f"xn_{tc-2}")
            em.sig(Sq, lambda e, tc=tc, xs=xs: e.dma_start(
                xs, dr["x"].ap()[tc * 128:(tc + 1) * 128, :]), evt=f"ldx_{tc}")
            em.wait(V, f"ldx_{tc}")
            st = sb["stat"]
            if tc > 0:
                em.wait(V, f"rstd_{tc-1}")
            em.op(V, lambda e, xs=xs: e.reduce_sum(st[:, 0:1], xs, axis=AX.X))
            em.op(V, lambda e, xs=xs: e.tensor_mul(sb["xsq"][:], xs, xs))
            em.op(V, lambda e: e.tensor_mul(st[:, 2:3], st[:, 0:1], st[:, 0:1]))
            em.op(V, lambda e: e.reduce_sum(st[:, 1:2], sb["xsq"][:], axis=AX.X))
            em.op(V, lambda e: e.tensor_scalar_mul(st[:, 3:4], st[:, 0:1], 1.0 / D))
            em.sig(V, lambda e: e.tensor_scalar(
                st[:, 5:6], st[:, 2:3], -1.0 / (D * D), sb["eps5"][:, 0:1],
                ALU.mult, ALU.add), evt=f"var_{tc}")
            em.wait(A, f"var_{tc}")
            em.op(A, lambda e: e.activation(st[:, 6:7], st[:, 1:2], ACTF.Ln,
                                            scale=1.0 / D, bias=st[:, 5:6]))
            em.sig(A, lambda e: e.activation(st[:, 7:8], st[:, 6:7], ACTF.Exp, scale=-0.5),
                   evt=f"rstd_{tc}")
            em.wait(V, f"rstd_{tc}")
            if tc > 0:
                em.wait(V, f"tr_{tc-1}_7")
            em.sig(V, lambda e, xs=xs: e.tensor_scalar(
                sb["xn"][:], xs, st[:, 3:4], st[:, 7:8], ALU.subtract, ALU.mult),
                evt=f"xn_{tc}")
            if DEBUG and tc == 0:
                em.wait(G, "xn_0")
                em.sig(G, lambda e: e.dma_start(dbg["xn0"].ap(), sb["xn"][:]),
                       evt="dbg_xn0")
                em.sig(G, lambda e: e.dma_start(dbg["xs0"].ap(), sb["xst"][:, 0:1024]),
                       evt="dbg_xs0")
                em.sig(G, lambda e: e.dma_start(dbg["st0"].ap(), sb["stat"][:]),
                       evt="dbg_st0")
            if DEBUG and tc == 1:
                em.wait(V, "dbg_xn0")
            if DEBUG and tc == 2:
                em.wait(Sq, "dbg_xs0")
            em.wait(T, f"xn_{tc}")
            if tc > 0:
                em.wait(T, f"p1ev_{tc-1}")
            for dc in range(DC):
                em.op(T, lambda e, dc=dc: e.transpose(
                    pbf(dc * 128, dc * 128 + 128), sb["xn"][:, dc * 128:dc * 128 + 128],
                    sb["idf32"][:]))
            em.sig(T, lambda e: e.matmul(pbf(3584, 3592, 0, 64), sb["onesb"][0:64, 0:64],
                                         sb["onesb"][0:64, 0:8], start=True, stop=True),
                   evt=f"tr_{tc}_7")
            if tc > 0:
                em.wait(V, f"mmg_{tc-1}")
            em.wait(V, f"tr_{tc}_7")
            for dc in range(DC):
                em.op(V, lambda e, dc=dc: e.tensor_copy(
                    sb["xnT32"][:, dc * 128:dc * 128 + 128], pbf(dc * 128, dc * 128 + 128)))
                em.sig(V, lambda e, dc=dc, tc=tc: e.tensor_copy(
                    sb["xnTb"][:, dc * 2048 + tc * 128: dc * 2048 + tc * 128 + 128],
                    pbf(dc * 128, dc * 128 + 128)),
                    evt=(f"p1ev_{tc}" if dc == DC - 1 else None))
            em.wait(T, f"p1ev_{tc}")
            if tc == 0:
                pass
            else:
                em.wait(T, f"gmask_{tc-1}")
            for dc in range(DC):
                em.sig(T, lambda e, dc=dc: e.matmul(
                    pbf(1024, 1032), sb["xnT32"][:, dc * 128:dc * 128 + 128],
                    sb["gwT32"][:, dc * 8:dc * 8 + 8], start=(dc == 0), stop=False))
            em.sig(T, lambda e: e.matmul(pbf(1024, 1032), sb["ones32"][:], sb["gbr"][:],
                                         start=False, stop=True), evt=f"mmg_{tc}")
            if tc == 0:
                em.wait(T, "ld_acat")
            else:
                em.wait(T, f"um_{tc-1}")
            for dc in range(DC):
                em.sig(T, lambda e, dc=dc, tc=tc: e.matmul(
                    pbf(1536, 1568),
                    sb["xnTb"][:, dc * 2048 + tc * 128: dc * 2048 + tc * 128 + 128],
                    sb["acatTb"][:, dc * 32:dc * 32 + 32],
                    start=(dc == 0), stop=(dc == DC - 1)),
                    evt=(f"mmu_{tc}" if dc == DC - 1 else None))
            em.wait(V, f"mmg_{tc}")
            if DEBUG and tc == TC - 1:
                em.sig(V, lambda e: e.tensor_copy(sb["dbgl"][:], pbf(1024, 1032)),
                       evt="dbg_glogcp")
            em.sig(V, lambda e: e.reduce_max(sb["stat"][:, 0:1], pbf(1024, 1032),
                                             axis=AX.X), evt=f"gmx_{tc}")
            em.wait(A, f"gmx_{tc}")
            em.sig(A, lambda e: e.copy(sb["stat"][:, 1:2], sb["stat"][:, 0:1]),
                   evt=f"gcp_{tc}")
            em.wait(V, f"gcp_{tc}")
            em.sig(V, lambda e: e.tensor_single_scalar(
                sb["gmask"][:], pbf(1024, 1032), sb["stat"][:, 1:2], ALU.is_ge),
                evt=f"gmask_{tc}")
            em.wait(A, f"gmask_{tc}")
            em.sig(A, lambda e: e.copy(sb["dbgl"][:], sb["gmask"][:]),
                   evt=f"gmcp_{tc}")
            em.wait(V, f"mmu_{tc}")
            if tc > 0:
                em.wait(V, f"mmut_{tc-1}")
            em.wait(V, f"gmcp_{tc}")
            for r in range(4):
                em.sig(V, lambda e, r=r: e.tensor_tensor(
                    bass.AP(umb_h, r, [[32, 128], [4, 8]]),
                    bass.AP(PB, 1536 + r, [[4096, 128], [4, 8]]),
                    sb["dbgl"][:],
                    ALU.mult), evt=(f"um_{tc}" if r == 3 else None))
            em.wait(T, f"um_{tc}")
            if tc > 0:
                em.wait(T, f"uT_{tc-1}")
            em.op(T, lambda e: e.transpose(pbb(4096, 4224, 0, 32), sb["umb"][:],
                                           sb["idbf"][:]))
            em.sig(T, lambda e: e.matmul(pbf(3584, 3592, 0, 64), sb["onesb"][0:64, 0:64],
                                         sb["onesb"][0:64, 0:8], start=True, stop=True),
                   evt=f"mmut_{tc}")
            em.wait(V, f"mmut_{tc}")
            em.sig(V, lambda e, tc=tc: e.tensor_copy(
                sb["uTb"][:, tc * 128:tc * 128 + 128], pbb(4096, 4224, 0, 32)),
                evt=f"uT_{tc}")
            if tc == 0:
                em.wait(T, "ld_wv"); em.wait(T, "ld_bv"); em.wait(T, "v_ones")
            else:
                em.wait(T, f"vev_{tc-1}")
            for nh in range(2):
                for dc in range(DC):
                    em.sig(T, lambda e, nh=nh, dc=dc, tc=tc: e.matmul(
                        pbf(2560 + nh * 512, 3072 + nh * 512),
                        sb["xnTb"][:, dc * 2048 + tc * 128: dc * 2048 + tc * 128 + 128],
                        sb["wvTb"][:, dc * 1024 + nh * 512: dc * 1024 + nh * 512 + 512],
                        start=(dc == 0), stop=False))
                em.sig(T, lambda e, nh=nh: e.matmul(
                    pbf(2560 + nh * 512, 3072 + nh * 512), sb["onesb"][0:1, 0:128],
                    sb["bvr"][:, nh * 512:nh * 512 + 512], start=False, stop=True),
                    evt=f"mmv_{tc}_{nh}")
            em.wait(V, f"mmv_{tc}_1")
            em.sig(V, lambda e, tc=tc: e.tensor_copy(
                bass.AP(vtile_h, tc * 1040, [[16640, 128], [65, 16], [1, 64]]),
                pbf(2560, 3584).rearrange("p (a b) -> p a b", a=16)),
                evt=f"vev_{tc}")

        if PH <= 1:
            em.sig(Sq, lambda e: e.dma_start(
                out_h.ap()[0:128, :],
                sb["xnTb"].bitcast(F32)[:, 0:2048]), evt="dummy_sto")
            em.wait(Sq, "dummy_sto", force=True)
            return
        if DEBUG:
            em.wait(G, "p1ev_15")
            em.wait(G, "rstd_15")
            em.wait(G, "dbg_glogcp")
            em.sig(G, lambda e: e.dma_start(dbg["glog"].ap(), sb["dbgl"][:]),
                   evt="dbg_glog")
            em.sig(G, lambda e: e.dma_start(dbg["stat"].ap(), sb["stat"][:]),
                   evt="dbg_stat")
            em.sig(G, lambda e: e.dma_start(dbg["xnT32"].ap(), sb["xnT32"][:]),
                   evt="dbg_xnT32")
            em.wait(G, "gmask_15")
            em.wait(G, "uT_15")
            em.sig(G, lambda e: e.dma_start(dbg["xnTb2"].ap(), sb["xnTb"][:]),
                   evt="dbg_early1")
            em.sig(G, lambda e: e.dma_start(dbg["gmask2"].ap(), sb["gmask"][:]),
                   evt="dbg_early2")
            em.sig(G, lambda e: e.dma_start(dbg["uT2"].ap(), sb["uTb"][:]),
                   evt="dbg_early3")
        last_mmB = None
        for h in range(NHEAD):
            par = h % 2
            if h == 0:
                em.wait(G, "xn_15")
                em.wait(G, "mmg_15")
            if h >= 2:
                em.wait(G, f"projdone_{h-2}")
            em.sig(G, lambda e, h=h, par=par: e.dma_start(
                sb["wqb"].rearrange("p (x a b) -> p x a b", x=2, a=DC)[:, par],
                bass.AP(dr["wqT"], h * 128, [[2048, 128], [128 * 2048, DC], [1, 128]])),
                evt=f"ldwq_{h}")
            em.sig(G, lambda e, h=h, par=par: e.dma_start(
                sb["wkb"].rearrange("p (x a b) -> p x a b", x=2, a=DC)[:, par],
                bass.AP(dr["wkT"], h * 128, [[2048, 128], [128 * 2048, DC], [1, 128]])),
                evt=f"ldwk_{h}")
            for mat, wb, rot, bc in (("q", "wqb", "qrotb", "bqc"), ("k", "wkb", "krotb", "bkc")):
                em.wait(T, f"ldw{mat}_{h}")
                if h == 0 and mat == "q":
                    em.wait(T, "p1ev_15")
                if h > 0 and mat == "q" and P2SUB == "full":
                    em.wait(T, f"m2s_{h-1}_3")
                    em.wait(T, f"att_{h-1}_3")
                for tt in range(TT):
                    for dc in range(DC):
                        em.sig(T, lambda e, dc=dc, tt=tt, wb=wb, par=par: e.matmul(
                            pbf(0, 512),
                            sb[wb][:, par * 1024 + dc * 128: par * 1024 + dc * 128 + 128],
                            sb["xnTb"][:, dc * 2048 + tt * 512: dc * 2048 + tt * 512 + 512],
                            start=(dc == 0), stop=(dc == DC - 1)),
                            evt=(f"mmA_{h}{mat}{tt}" if dc == DC - 1 else None))
                    em.wait(V, f"mmA_{h}{mat}{tt}")
                    if P2SUB == "projm":
                        em.sig(V, lambda e, rot=rot, tt=tt: e.tensor_copy(
                            sb[rot][:, tt * 512:tt * 512 + 512], pbf(0, 512)),
                            evt=f"evA_{h}{mat}{tt}")
                        em.sig(V, lambda e: None, evt=f"rope_{h}{mat}{tt}")
                        last_mmB = None
                        continue
                    if last_mmB is not None:
                        em.wait(V, last_mmB)
                    em.sig(V, lambda e, bc=bc, h=h: e.tensor_scalar_add(
                        sb["qsb"][:], pbf(0, 512), sb[bc][:, h:h + 1]),
                        evt=f"evA_{h}{mat}{tt}")
                    if P2SUB == "proja":
                        em.sig(V, lambda e, rot=rot, tt=tt: e.tensor_copy(
                            sb[rot][:, tt * 512:tt * 512 + 512], sb["qsb"][:]),
                            evt=f"rope_{h}{mat}{tt}")
                        last_mmB = None
                        continue
                    em.wait(T, f"evA_{h}{mat}{tt}")
                    em.sig(T, lambda e: e.matmul(pbf(512, 1024), sb["psig"][:], sb["qsb"][:],
                                                 start=True, stop=True),
                           evt=f"mmB_{h}{mat}{tt}")
                    last_mmB = f"mmB_{h}{mat}{tt}"
                    if P2SUB == "projb":
                        em.wait(V, f"mmB_{h}{mat}{tt}")
                        em.sig(V, lambda e, rot=rot, tt=tt: e.tensor_copy(
                            sb[rot][:, tt * 512:tt * 512 + 512], pbf(512, 1024)),
                            evt=f"rope_{h}{mat}{tt}")
                        continue
                    if h > 0 and mat == "q" and tt == 0 and P2SUB != "proj":
                        em.wait(V, f"mmS_{h-1}_3_7")
                    em.op(V, lambda e, tt=tt: e.tensor_mul(
                        sb["t1b"][:], sb["qsb"][:], sb["coss"][:, tt * 512:tt * 512 + 512]))
                    em.wait(V, f"mmB_{h}{mat}{tt}")
                    em.op(V, lambda e, tt=tt: e.tensor_mul(
                        sb["t2b"][:], pbf(512, 1024), sb["sins"][:, tt * 512:tt * 512 + 512]))
                    em.sig(V, lambda e, rot=rot, tt=tt: e.tensor_add(
                        sb[rot][:, tt * 512:tt * 512 + 512], sb["t1b"][:], sb["t2b"][:]),
                        evt=f"rope_{h}{mat}{tt}")
            em.alias(f"projdone_{h}", f"mmB_{h}k3")
            if P2SUB == "proj":
                continue
            em.wait(T, f"rope_{h}q3")
            em.wait(T, f"rope_{h}k3")

            def emit_attnv(qt, kcp, kcmax, final):
                kc0 = 2 * kcp
                slot = kcp % 2
                em.wait(T, f"mask_{h}_{qt}_{kcp}")
                if kcp == 0:
                    if qt > 0:
                        em.wait(T, f"vcomb_{h}_{qt-1}")
                    elif h > 0:
                        em.wait(T, f"vcomb_{h-1}_3")
                    else:
                        em.wait(T, "vev_15")
                for mi in range(2):
                    ec0 = slot * 2048 + mi * 1024
                    pc = 3072 + mi * 512
                    for j, kc in ((0, kc0), (1, kc0 + 1)):
                        em.sig(T, lambda e, kc=kc, j=j, pc=pc, ec0=ec0: e.matmul(
                            pbf(pc, pc + 512, 0, 65),
                            sb["vtile"][:, kc * 1040 + h * 65: kc * 1040 + h * 65 + 65],
                            sb["ecat"][:, ec0 + j * 512: ec0 + j * 512 + 512],
                            start=(kc == 0), stop=(kc == kcmax - 1)),
                            evt=(f"mmPp_{h}_{qt}_{kcp}" if (mi == 1 and j == 1) else None))

            for qt in range(TT):
                kcmax = 4 * qt + 4
                npair = kcmax // 2
                for kcp in range(npair):
                    kc0 = 2 * kcp
                    slot = kcp % 2
                    if kcp > 0:
                        em.wait(T, f"exp_{h}_{qt}_{kcp-1}")
                    elif qt > 0:
                        em.wait(T, f"exp_{h}_{qt-1}_{(4*(qt-1)+4)//2-1}")
                    elif h > 0:
                        em.wait(T, f"exp_{h-1}_3_7")
                    else:
                        em.wait(T, "gmask_15"); em.wait(T, "um_15")
                        em.wait(T, "uT_15"); em.wait(T, "vev_15")
                    for mi in range(2):
                        base = mi * 64
                        scol = 1024 + mi * 1024
                        for j, kc in ((0, kc0), (1, kc0 + 1)):
                            em.sig(T, lambda e, base=base, scol=scol, j=j, kc=kc, qt=qt: e.matmul(
                                pbf(scol + j * 512, scol + j * 512 + 512),
                                sb["krotb"][base:base + 64, kc * 128:kc * 128 + 128],
                                sb["qrotb"][base:base + 64, qt * 512:qt * 512 + 512],
                                start=True, stop=True),
                                evt=(f"mmS_{h}_{qt}_{kcp}" if (mi == 1 and j == 1) else None))
                    em.wait(A, f"mmS_{h}_{qt}_{kcp}")
                    if P2SUB == "scores":
                        em.wait(A, f"mask_{h}_{qt}_{kcp-1}") if kcp > 0 else None
                    elif kcp >= 2:
                        em.wait(A, f"mmPp_{h}_{qt}_{kcp-2}")
                    elif qt > 0:
                        em.wait(A, f"mmPp_{h}_{qt-1}_{(4*(qt-1)+4)//2-1}")
                    elif h > 0:
                        em.wait(A, f"mmPp_{h-1}_3_7")
                    em.sig(A, lambda e, slot=slot: e.activation(
                        sb["ecat"][:, slot * 2048: slot * 2048 + 2048], pbf(1024, 3072),
                        ACTF.Exp, scale=0.125), evt=f"exp_{h}_{qt}_{kcp}")
                    em.wait(V, f"exp_{h}_{qt}_{kcp}")
                    mops = []
                    for j, kc in ((0, kc0), (1, kc0 + 1)):
                        doff = kc - 4 * qt
                        if doff >= 0:
                            mc = doff * 128
                            for mi in range(2):
                                col0 = slot * 2048 + mi * 1024 + j * 512
                                if mc > 0:
                                    mops.append(("ms", col0, mc))
                                mops.append(("tm", col0, mc))
                    for i, (kind, col0, mc) in enumerate(mops):
                        last = (i == len(mops) - 1)
                        ev = f"mask_{h}_{qt}_{kcp}" if last else None
                        if kind == "ms":
                            em.sig(V, lambda e, col0=col0, mc=mc: e.tensor_copy(
                                sb["ecat"][:, col0: col0 + mc], sb["zerob"][:, 0:mc]), evt=ev)
                        else:
                            em.sig(V, lambda e, col0=col0, mc=mc: e.tensor_mul(
                                sb["ecat"][:, col0 + mc: col0 + mc + 128],
                                sb["ecat"][:, col0 + mc: col0 + mc + 128],
                                sb["trim"][:]), evt=ev)
                    if not mops:
                        em.alias(f"mask_{h}_{qt}_{kcp}", f"exp_{h}_{qt}_{kcp}")
                    if P2SUB != "scores" and kcp > 0:
                        emit_attnv(qt, kcp - 1, kcmax, False)
                if P2SUB == "scores":
                    continue
                emit_attnv(qt, npair - 1, kcmax, True)
                lastp = npair - 1

                em.wait(V, f"mmPp_{h}_{qt}_{lastp}")
                prev = (h, qt - 1) if qt > 0 else (h - 1, 3)
                if prev[0] >= 0:
                    em.wait(V, f"bcRS_{prev[0]}_{prev[1]}")
                    em.wait(V, f"rs_{prev[0]}_{prev[1]}")
                em.op(V, lambda e: e.tensor_copy(sb["rbf"][64:65, 0:512],
                                                 pbf(3584, 4096, 64, 65)))
                em.op(V, lambda e: e.scalar_tensor_tensor(
                    sb["rtmp"][64:65, 0:512], pbf(3072, 3584, 64, 65), 1e-3,
                    sb["rbf"][64:65, 0:512], ALU.mult, ALU.mult))
                em.wait(V, "lam_bc")
                em.op(V, lambda e, h=h: e.tensor_scalar_mul(
                    sb["rbf"][64:65, 512:1024], pbf(3072, 3584, 64, 65),
                    sb["lbc"][64:65, h:h + 1]))
                em.sig(V, lambda e: e.tensor_mul(
                    sb["rtmp"][64:65, 512:1024], sb["rtmp"][64:65, 0:512],
                    sb["rtmp"][64:65, 0:512]), evt=f"rows_{h}_{qt}")
                em.wait(T, f"rows_{h}_{qt}")
                if prev[0] >= 0:
                    em.wait(T, f"m2s_{prev[0]}_{prev[1]}")
                    em.wait(T, f"att_{prev[0]}_{prev[1]}")
                em.sig(T, lambda e: e.matmul(pbf(0, 512, 0, 64), sb["onesb"][64:65, 0:64],
                                             sb["rbf"][64:65, 0:512], start=True, stop=True),
                       evt=f"bcA_{h}_{qt}")
                em.sig(T, lambda e: e.matmul(pbf(512, 1024, 0, 64), sb["onesb"][64:65, 0:64],
                                             sb["rbf"][64:65, 512:1024], start=True, stop=True),
                       evt=f"bcB_{h}_{qt}")
                em.wait(V, f"bcB_{h}_{qt}")
                em.op(V, lambda e: e.tensor_copy(sb["ct"][0:64, :], pbf(0, 512, 0, 64)))
                em.op(V, lambda e: e.tensor_mul(sb["cv"][0:64, :], pbf(3072, 3584, 0, 64),
                                                sb["ct"][0:64, :]))
                em.op(V, lambda e: e.tensor_copy(sb["ct"][0:64, :], pbf(512, 1024, 0, 64)))
                em.op(V, lambda e: e.tensor_mul(sb["m2s"][0:64, :], pbf(3584, 4096, 0, 64),
                                                sb["ct"][0:64, :]))
                em.sig(V, lambda e: e.tensor_sub(sb["cv"][0:64, :], sb["cv"][0:64, :],
                                                 sb["m2s"][0:64, :]), evt=f"vcomb_{h}_{qt}")
                em.sig(V, lambda e: e.tensor_mul(sb["v2b"][0:64, :], sb["cv"][0:64, :],
                                                 sb["cv"][0:64, :]), evt=f"v2_{h}_{qt}")
                em.wait(T, f"v2_{h}_{qt}")
                em.sig(T, lambda e: e.matmul(pbf(0, 512, 0, 64), sb["onesb"][0:64, 0:64],
                                             sb["v2b"][0:64, :], start=True, stop=True),
                       evt=f"mmM2_{h}_{qt}")
                em.wait(V, f"mmM2_{h}_{qt}")
                em.sig(V, lambda e: e.tensor_copy(sb["m2s"][64:65, :], pbf(0, 512, 0, 1)),
                       evt=f"m2s_{h}_{qt}")
                em.sig(V, lambda e: e.scalar_tensor_tensor(
                    sb["zt"][64:65, 0:512], sb["m2s"][64:65, :], 1.0 / 64,
                    sb["rtmp"][64:65, 512:1024], ALU.mult, ALU.add), evt=f"z_{h}_{qt}")
                em.wait(A, f"z_{h}_{qt}")
                em.op(A, lambda e: e.activation(sb["zt"][64:65, 512:1024],
                                                sb["zt"][64:65, 0:512], ACTF.Ln))
                em.sig(A, lambda e: e.activation(sb["rsb"][64:65, :],
                                                 sb["zt"][64:65, 512:1024], ACTF.Exp,
                                                 scale=-0.5), evt=f"rs_{h}_{qt}")
                em.wait(T, f"rs_{h}_{qt}")
                em.sig(T, lambda e: e.matmul(pbf(512, 1024, 0, 64), sb["onesb"][64:65, 0:64],
                                             sb["rsb"][64:65, 0:512], start=True, stop=True),
                       evt=f"bcRS_{h}_{qt}")
                em.wait(V, f"bcRS_{h}_{qt}")
                hc, hodd = h // 2, h % 2
                acol = hc * 2048 + qt * 512
                if hodd == 0:
                    em.sig(V, lambda e, acol=acol: e.tensor_mul(
                        sb["attoT"][0:64, acol:acol + 512], sb["cv"][0:64, :],
                        pbf(512, 1024, 0, 64)), evt=f"att_{h}_{qt}")
                else:
                    em.op(V, lambda e: e.tensor_mul(
                        sb["atmp"][0:64, :], sb["cv"][0:64, :], pbf(512, 1024, 0, 64)))
                    em.sig(V, lambda e, acol=acol: e.tensor_copy(
                        sb["attoT"][64:128, acol:acol + 512], sb["atmp"][0:64, :]),
                        evt=f"att_{h}_{qt}")

        if PH <= 2:
            em.sig(Sq, lambda e: e.dma_start(
                out_h.ap()[0:128, :],
                sb["attoT"].bitcast(F32)[:, 0:2048]), evt="dummy_sto")
            em.wait(Sq, "dummy_sto", force=True)
            return
        for dc in range(DC):
            par = dc % 2
            if dc >= 2:
                em.wait(G, f"mmST_{dc-2}_3")
            em.sig(G, lambda e, dc=dc, par=par: e.dma_start(
                sb["wost"].rearrange("p (x a b) -> p x a b", x=2, a=DC)[:, par],
                bass.AP(dr["woT"], dc * 128, [[D, 128], [128 * D, DC], [1, 128]])),
                evt=f"ldwo_{dc}")
            em.wait(V, f"ldwo_{dc}")
            if dc >= 2:
                em.wait(V, f"mmST_{dc-2}_3")
            for dcin in range(DC):
                em.sig(V, lambda e, dcin=dcin, par=par: e.tensor_scalar_mul(
                    sb["wost"][:, par * 1024 + dcin * 128: par * 1024 + dcin * 128 + 128],
                    sb["wost"][:, par * 1024 + dcin * 128: par * 1024 + dcin * 128 + 128],
                    sb["hnwc"][:, dcin:dcin + 1]),
                    evt=(f"wosc_{dc}" if dcin == DC - 1 else None))
            for tt in range(TT):
                idx = dc * TT + tt
                ppar = idx % 2
                em.wait(T, f"att_{H-1}_3")
                em.wait(T, f"exp_{H-1}_3_7")
                em.wait(T, f"wosc_{dc}")
                if idx >= 2:
                    pdc, ptt = (idx - 2) // TT, (idx - 2) % TT
                    em.wait(T, f"sev_{pdc}_{ptt}")
                for dcin in range(DC):
                    em.sig(T, lambda e, dcin=dcin, tt=tt, par=par, ppar=ppar: e.matmul(
                        pbf(1024 + ppar * 512, 1536 + ppar * 512),
                        sb["wost"][:, par * 1024 + dcin * 128: par * 1024 + dcin * 128 + 128],
                        sb["attoT"][:, dcin * 2048 + tt * 512: dcin * 2048 + tt * 512 + 512],
                        start=(dcin == 0), stop=(dcin == DC - 1)),
                        evt=(f"mmST_{dc}_{tt}" if dcin == DC - 1 else None))
                em.wait(V, f"mmST_{dc}_{tt}")
                em.sig(V, lambda e, dc=dc, tt=tt, ppar=ppar: e.tensor_scalar_add(
                    sb["sTb"][:, dc * 2048 + tt * 512: dc * 2048 + tt * 512 + 512],
                    pbf(1024 + ppar * 512, 1536 + ppar * 512), sb["boc"][:, dc:dc + 1]),
                    evt=f"sev_{dc}_{tt}")

        if PH <= 3:
            em.sig(Sq, lambda e: e.dma_start(
                out_h.ap()[0:128, :],
                sb["sTb"].bitcast(F32)[:, 0:2048]), evt="dummy_sto")
            em.wait(Sq, "dummy_sto", force=True)
            return
        for tt in range(TT):
            for fc in range(FC):
                idx = tt * FC + fc
                par = idx % 2
                if idx == 0:
                    em.wait(G, f"mmST_{DC-1}_3")
                    em.wait(G, f"att_{H-1}_3")
                    em.wait(G, f"rs_{H-1}_3")
                if idx >= 2:
                    em.wait(G, f"mmI_{idx-2}")
                em.sig(G, lambda e, fc=fc, par=par: e.dma_start(
                    sb["wib"].rearrange("p (x a b) -> p x a b", x=2, a=DC)[:, par],
                    bass.AP(dr["wiT"], fc * 128, [[F, 128], [128 * F, DC], [1, 128]])),
                    evt=f"ldwi_{idx}")
                em.wait(T, f"ldwi_{idx}")
                if idx == 0:
                    em.wait(T, "ld_bct")
                    em.wait(T, "uT_15")
                    em.wait(T, f"m2s_{H-1}_3")
                    em.wait(T, f"att_{H-1}_3")
                if idx >= 2:
                    em.wait(T, f"iev_{idx-2}")
                for dc in range(DC):
                    em.sig(T, lambda e, dc=dc, tt=tt, par=par: e.matmul(
                        pbf(par * 512, par * 512 + 512),
                        sb["wib"][:, par * 1024 + dc * 128: par * 1024 + dc * 128 + 128],
                        sb["xnTb"][:, dc * 2048 + tt * 512: dc * 2048 + tt * 512 + 512],
                        start=(dc == 0), stop=False))
                em.sig(T, lambda e, fc=fc, tt=tt, par=par: e.matmul(
                    pbf(par * 512, par * 512 + 512),
                    sb["bctb"][0:32, fc * 128:fc * 128 + 128],
                    sb["uTb"][0:32, tt * 512:tt * 512 + 512],
                    start=False, stop=True), evt=f"mmI_{idx}")
                em.wait(V, f"mmI_{idx}")
                if idx == 0:
                    em.wait(V, f"mmPp_{H-1}_3_7")
                if tt > 0:
                    em.wait(V, f"mmG_{(tt-1)*DC + DC - 1}")
                em.sig(V, lambda e, fc=fc, par=par: e.tensor_scalar(
                    sb["interTb"][:, fc * 512:fc * 512 + 512],
                    pbf(par * 512, par * 512 + 512),
                    sb["wibc"][:, fc:fc + 1], 0.0, ALU.add, ALU.max), evt=f"iev_{idx}")
            for dc in range(DC):
                gidx = tt * DC + dc
                par = gidx % 2
                if gidx >= 2:
                    em.wait(G, f"mmG_{gidx-2}")
                em.sig(G, lambda e, dc=dc, par=par: e.dma_start(
                    sb["fwoTb"].rearrange("p (x a b) -> p x a b", x=2, a=FC)[:, par],
                    bass.AP(dr["fwoT"], dc * 128, [[D, 128], [128 * D, FC], [1, 128]])),
                    evt=f"ldfw_{gidx}")
                if gidx == 0:
                    em.wait(Sq, f"z_{H-1}_3")
                if gidx >= 2:
                    em.wait(Sq, f"oev_{gidx-2}")
                em.sig(Sq, lambda e, dc=dc, tt=tt, par=par: e.dma_start(
                    sb["xTs"][:, par * 512:(par + 1) * 512],
                    dr["xT"].ap()[dc * 128:(dc + 1) * 128, tt * 512:(tt + 1) * 512]),
                    evt=f"ldxT_{gidx}")
                em.wait(T, f"ldfw_{gidx}")
                em.wait(T, f"iev_{tt * FC + FC - 1}")
                if gidx >= 2:
                    em.wait(T, f"gev_{gidx-2}")
                for fc in range(FC):
                    em.sig(T, lambda e, fc=fc, par=par: e.matmul(
                        pbf(2048 + par * 512, 2560 + par * 512),
                        sb["fwoTb"][:, par * 4096 + fc * 128: par * 4096 + fc * 128 + 128],
                        sb["interTb"][:, fc * 512:fc * 512 + 512],
                        start=(fc == 0), stop=(fc == FC - 1)),
                        evt=(f"mmG_{gidx}" if fc == FC - 1 else None))
                em.wait(A, f"mmG_{gidx}")
                if gidx >= 2:
                    em.wait(A, f"oev_{gidx-2}")
                em.sig(A, lambda e, dc=dc, par=par: e.activation(
                    sb["gbuf"][:, par * 512:(par + 1) * 512],
                    pbf(2048 + par * 512, 2560 + par * 512), ACTF.Silu,
                    bias=sb["wobc"][:, dc:dc + 1]), evt=f"gev_{gidx}")
                em.wait(V, f"gev_{gidx}")
                em.wait(V, f"ldxT_{gidx}")
                if gidx == 0:
                    em.wait(V, f"bcB_{H-1}_3")
                if gidx >= 2:
                    em.wait(V, f"sto_{gidx-2}")
                em.op(V, lambda e, dc=dc, tt=tt, par=par: e.tensor_mul(
                    sb["tbuf"][:], sb["sTb"][:, dc * 2048 + tt * 512: dc * 2048 + tt * 512 + 512],
                    sb["gbuf"][:, par * 512:(par + 1) * 512]))
                em.sig(V, lambda e, par=par: e.tensor_add(
                    sb["outb"][:, par * 512:(par + 1) * 512], sb["tbuf"][:],
                    sb["xTs"][:, par * 512:(par + 1) * 512]), evt=f"oev_{gidx}")
                em.wait(Sq, f"oev_{gidx}")
                em.sig(Sq, lambda e, dc=dc, tt=tt, par=par: e.dma_start(
                    out_h.ap()[dc * 128:(dc + 1) * 128, tt * 512:(tt + 1) * 512],
                    sb["outb"][:, par * 512:(par + 1) * 512]), evt=f"sto_{gidx}")

        if DEBUG:
            em.wait(G, f"sto_{TT * DC - 1}")
            dbg_list = [("xnTb", "xnTb"), ("v", "vtile"), ("uT", "uTb"),
                        ("qrot", "qrotb"), ("krot", "krotb"), ("att", "attoT"),
                        ("sT", "sTb"), ("inter", "interTb"), ("gmask", "gmask")]
            for nm, src in dbg_list:
                em.sig(G, lambda e, nm=nm, src=src: e.dma_start(
                    dbg[nm].ap(), sb[src][:]), evt=f"dbg_{nm}")
            em.wait(G, f"dbg_{dbg_list[-1][0]}", force=True)
        for gidx in range(TT * DC - NDMASEM, TT * DC):
            em.wait(Sq, f"sto_{gidx}", force=True)

    with es:
        block = es.enter_context(nc.Block())
        prog(Em("count", sems, events))

        @block.tensor
        def _(eng):
            prog(Em("T", sems, events, eng))

        @block.vector
        def _(eng):
            prog(Em("V", sems, events, eng))

        @block.scalar
        def _(eng):
            prog(Em("A", sems, events, eng))

        @block.gpsimd
        def _(eng):
            prog(Em("G", sems, events, eng))

        @block.sync
        def _(eng):
            prog(Em("S", sems, events, eng))

    return nc


_NC_CACHE = None


def kernel(**inputs):
    global _NC_CACHE
    consts = _host_consts()
    w = _prep_weights(inputs)
    x = np.asarray(inputs["L0_attn_x"], np.float32)
    if _NC_CACHE is None:
        _NC_CACHE = build_nc()
    nc = _NC_CACHE
    in_maps = []
    for c in range(NCORE):
        m = {}
        m.update(consts)
        m.update(w)
        xc = np.ascontiguousarray(x[c])
        m["x"] = xc
        m["xT"] = np.ascontiguousarray(xc.T)
        in_maps.append(m)
    res = run_bass_kernel_spmd(nc, in_maps, core_ids=list(range(NCORE)))
    out = np.stack([np.ascontiguousarray(r["outT"].T) for r in res.results], axis=0)
    return out
